# version 01
# baseline (speedup 1.0000x reference)
import contextlib
import numpy as np
import ml_dtypes
import concourse.bass as bass
import concourse.mybir as mybir
from concourse.bass_utils import run_bass_kernel_spmd

F32 = mybir.dt.float32
BF16 = mybir.dt.bfloat16
AF = mybir.ActivationFunctionType
ALU = mybir.AluOpType
AX = mybir.AxisListType

D = 1024
DI = 2048
DS = 128
NH = 32
HD = 64
CONVD = 2304
RIN = 4384
NE = 8
INTER = 1024
T = 2048
NTILE = 16
NG = 4
GS = 512
EPS = 1e-5

_ENGS = ("t", "v", "s", "g", "y")
_ENG_ATTR = {"t": "tensor", "v": "vector", "s": "scalar", "g": "gpsimd", "y": "sync"}


class Prog:
    def __init__(self, nc, es):
        self.nc = nc
        self.q = {e: [] for e in _ENGS}
        self.sem = {}
        self.cnt = {}
        for name in ("ld", "st", "T", "V", "S", "G"):
            self.sem[name] = es.enter_context(nc.semaphore(name))
            self.cnt[name] = 0

    def emit(self, e, f):
        self.q[e].append(f)

    def inc(self, e, f, sname=None, n=None):
        if sname is None:
            sname = {"t": "T", "v": "V", "s": "S", "g": "G", "y": "ld"}[e]
        if n is None:
            n = 16 if sname in ("ld", "st") else 1
        self.cnt[sname] += n
        sem = self.sem[sname]
        if e in ("v", "s") or (e == "g" and sname == "G"):
            self.q[e].append(lambda eng, f=f: f(eng))
            self.q[e].append(lambda eng, sem=sem, n=n: eng.drain().then_inc(sem, n))
        else:
            self.q[e].append(lambda eng, f=f, sem=sem, n=n: f(eng).then_inc(sem, n))
        return self.cnt[sname]

    def incd(self, e, f, sname=None, n=None):
        if sname is None:
            sname = {"t": "T", "v": "V", "s": "S", "g": "G", "y": "ld"}[e]
        if n is None:
            n = 16 if sname in ("ld", "st") else 1
        self.q[e].append(lambda eng, f=f: f(eng))
        self.cnt[sname] += n
        sem = self.sem[sname]
        self.q[e].append(lambda eng, sem=sem, n=n: eng.drain().then_inc(sem, n))
        return self.cnt[sname]

    def wait(self, e, sname, val):
        if val <= 0:
            return
        sem = self.sem[sname]
        self.q[e].append(lambda eng, sem=sem, val=val: eng.wait_ge(sem, val))

    def barrier(self):
        marks = {}
        for e, sname in (("t", "T"), ("v", "V"), ("s", "S"), ("g", "G")):
            marks[sname] = self.inc(e, lambda eng: eng.nop(), sname=sname, n=1)
        marks["ld"] = self.cnt["ld"]
        marks["st"] = self.cnt["st"]
        for e in _ENGS:
            for sname, val in marks.items():
                self.wait(e, sname, val)

    def build(self, block):
        for e in _ENGS:
            fs = self.q[e]
            getattr(block, _ENG_ATTR[e])(lambda eng, fs=fs: [f(eng) for f in fs])


def bcast_ap(sb_ap, reps, rep_inner):
    return sb_ap.unsqueeze(2).broadcast_to([sb_ap.shape[0], reps, rep_inner])


def _fm(v, nchunk):
    return np.ascontiguousarray(v.reshape(nchunk, 128).T)


def prep_weights(i):
    w = {}
    w["W_inT"] = np.ascontiguousarray(i["L0_attn_W_in"].T)
    w["conv_w_fm"] = np.ascontiguousarray(
        np.asarray(i["L0_attn_conv_w"]).reshape(18, 128, 4).transpose(1, 0, 2).reshape(128, 72))
    w["conv_b_fm"] = _fm(np.asarray(i["L0_attn_conv_b"]), 18)
    w["A_log_r"] = np.asarray(i["L0_attn_A_log"]).reshape(1, NH)
    w["dt_bias_r"] = np.asarray(i["L0_attn_dt_bias"]).reshape(1, NH)
    w["D_r"] = np.asarray(i["L0_attn_D"]).reshape(1, NH)
    w["norm_w_r"] = np.asarray(i["L0_attn_norm_w"]).reshape(1, DI)
    w["W_outT"] = np.ascontiguousarray(i["L0_attn_W_out"].T)
    w["gate_wT"] = np.ascontiguousarray(i["L0_ffn_gate_w"].T)
    w["moe_w1"] = np.asarray(i["L0_ffn_w1"])
    w["moe_w2"] = np.asarray(i["L0_ffn_w2"])
    w["moe_w3"] = np.asarray(i["L0_ffn_w3"])
    w["moe_b1_fm"] = np.stack([_fm(np.asarray(i["L0_ffn_b1"])[e], 8) for e in range(NE)])
    w["moe_b3_fm"] = np.stack([_fm(np.asarray(i["L0_ffn_b3"])[e], 8) for e in range(NE)])
    w["moe_b2"] = np.asarray(i["L0_ffn_b2"])
    w["sw1"] = np.asarray(i["L0_ffn_sw1"])
    w["sw2"] = np.asarray(i["L0_ffn_sw2"])
    w["sw3"] = np.asarray(i["L0_ffn_sw3"])
    w["sb1_fm"] = _fm(np.asarray(i["L0_ffn_sb1"]), 8)
    w["sb3_fm"] = _fm(np.asarray(i["L0_ffn_sb3"]), 8)
    w["sb2_r"] = np.asarray(i["L0_ffn_sb2"]).reshape(1, INTER)
    w["Wk1"] = np.asarray(i["L1_attn_Wk"])
    w["Wq1"] = np.asarray(i["L1_attn_Wq"])
    w["Wo1"] = np.asarray(i["L1_attn_Wo"])
    w["bo_r"] = np.asarray(i["L1_attn_bo"]).reshape(1, D)
    w["WrT"] = np.ascontiguousarray(i["L1_ffn_W_r"].T)
    w["WkT"] = np.ascontiguousarray(i["L1_ffn_W_k"].T)
    w["WvT"] = np.ascontiguousarray(i["L1_ffn_W_v"].T)
    w["mu_r_fm"] = _fm(np.asarray(i["L1_ffn_mu_r"]), 8)
    w["mu_k_fm"] = _fm(np.asarray(i["L1_ffn_mu_k"]), 8)
    w["ident32"] = np.eye(128, dtype=np.float32)
    w["identbf"] = np.eye(128).astype(ml_dtypes.bfloat16)
    w["U128"] = np.triu(np.ones((128, 128), np.float32))
    w["SLT"] = np.tril(np.ones((128, 128)), -1).astype(ml_dtypes.bfloat16)
    w["SLT32"] = np.tril(np.ones((128, 128), np.float32), -1)
    w["TRIU32"] = np.triu(np.ones((128, 128), np.float32))
    w["ones_row"] = np.ones((1, 128), np.float32)
    w["ones_col"] = np.ones((128, 1), np.float32)
    return {k: (v if v.dtype in (np.dtype(np.float32), np.dtype(ml_dtypes.bfloat16))
                else v.astype(np.float32)) for k, v in w.items()}


class Common:
    pass


def emit_prelude(c, io, es, need=()):
    nc = c.nc
    h = Common()
    h.psB = es.enter_context(nc.psum_tensor("psB", [128, 2048], F32))
    h.psA = es.enter_context(nc.psum_tensor("psA", [128, 1024], F32))
    h.psTb = es.enter_context(nc.psum_tensor("psTb", [128, 1024], BF16))
    h.id32 = es.enter_context(nc.sbuf_tensor("c_id32", [128, 128], F32))
    h.idbf = es.enter_context(nc.sbuf_tensor("c_idbf", [128, 128], BF16))
    h.ones_row = es.enter_context(nc.sbuf_tensor("c_ones_row", [1, 128], F32))
    h.ones_col = es.enter_context(nc.sbuf_tensor("c_ones_col", [128, 1], F32))
    h.kv = es.enter_context(nc.sbuf_tensor("c_kv", [128, 4], F32))
    vv0 = c.inc("v", lambda eng: eng.memset(h.kv[:, 0:1], EPS))
    vv0 = c.inc("v", lambda eng: eng.memset(h.kv[:, 1:2], 1.0))
    vv0 = c.inc("v", lambda eng: eng.memset(h.kv[:, 2:3], -20.0))
    c.wait("s", "V", vv0)
    loads = [(h.id32, "ident32"), (h.idbf, "identbf"),
             (h.ones_row, "ones_row"), (h.ones_col, "ones_col")]
    if "mamba" in need:
        h.U128 = es.enter_context(nc.sbuf_tensor("c_U128", [128, 128], F32))
        h.SLT = es.enter_context(nc.sbuf_tensor("c_SLT", [128, 128], BF16))
        h.TRIU32 = es.enter_context(nc.sbuf_tensor("c_TRIU32", [128, 128], F32))
        h.SLT32 = es.enter_context(nc.sbuf_tensor("c_SLT32", [128, 128], F32))
        h.convw = es.enter_context(nc.sbuf_tensor("c_convw", [128, 72], F32))
        h.convb = es.enter_context(nc.sbuf_tensor("c_convb", [128, 18], F32))
        h.row32 = es.enter_context(nc.sbuf_tensor("c_row32", [1, 32], F32))
        h.row32b = es.enter_context(nc.sbuf_tensor("c_row32b", [1, 32], F32))
        h.Aneg_b = es.enter_context(nc.sbuf_tensor("c_Aneg_b", [128, 32], F32))
        h.dtb_b = es.enter_context(nc.sbuf_tensor("c_dtb_b", [128, 32], F32))
        loads += [(h.U128, "U128"), (h.SLT, "SLT"), (h.TRIU32, "TRIU32"),
                  (h.SLT32, "SLT32"),
                  (h.convw, "conv_w_fm"), (h.convb, "conv_b_fm")]
    if "mamba_full" in need:
        h.D_b = es.enter_context(nc.sbuf_tensor("c_D_b", [128, 32], F32))
        h.normw_b = es.enter_context(nc.sbuf_tensor("c_normw_b", [128, DI], F32))
        h.rowDI = es.enter_context(nc.sbuf_tensor("c_rowDI", [1, DI], F32))
    if "moe" in need:
        h.sb2_b = es.enter_context(nc.sbuf_tensor("c_sb2_b", [128, INTER], F32))
        h.rowI = es.enter_context(nc.sbuf_tensor("c_rowI", [1, INTER], F32))
    if "l1" in need:
        h.bo_b = es.enter_context(nc.sbuf_tensor("c_bo_b", [128, D], F32))
        h.rowD = es.enter_context(nc.sbuf_tensor("c_rowD", [1, D], F32))
        h.mur = es.enter_context(nc.sbuf_tensor("c_mur", [128, 8], F32))
        h.muk = es.enter_context(nc.sbuf_tensor("c_muk", [128, 8], F32))
        h.omur = es.enter_context(nc.sbuf_tensor("c_omur", [128, 8], F32))
        h.omuk = es.enter_context(nc.sbuf_tensor("c_omuk", [128, 8], F32))
        loads += [(h.mur, "mu_r_fm"), (h.muk, "mu_k_fm")]

    for sb, name in loads:
        c.inc("y", lambda eng, sb=sb, ap=io[name]: eng.dma_start(out=sb[:], in_=ap))
    ldv = c.cnt["ld"]
    for e in ("t", "v", "s"):
        c.wait(e, "ld", ldv)

    def bc_row(src_row_ap, ncols, dst_sb_ap):
        tv = None
        for s0 in range(0, ncols, 512):
            s1 = min(s0 + 512, ncols)
            tv = c.inc("t", lambda eng, a=src_row_ap[0:1, s0:s1], o=h.psB[:, s0:s1]:
                       eng.matmul(out=o, lhsT=h.ones_row[:], rhs=a, start=True, stop=True))
        c.wait("v", "T", tv)
        vv = c.inc("v", lambda eng, o=dst_sb_ap[:, 0:ncols], i=h.psB[:, 0:ncols]:
                   eng.tensor_copy(out=o, in_=i))
        c.wait("t", "V", vv)

    if "mamba" in need:
        c.inc("y", lambda eng: eng.dma_start(out=h.row32[:], in_=io["A_log_r"]))
        c.inc("y", lambda eng: eng.dma_start(out=h.row32b[:], in_=io["dt_bias_r"]))
        for e in ("t", "v", "s"):
            c.wait(e, "ld", c.cnt["ld"])
        bc_row(h.row32[:], 32, h.Aneg_b[:])
        bc_row(h.row32b[:], 32, h.dtb_b[:])
        sv = c.inc("s", lambda eng: eng.activation(out=h.Aneg_b[:], in_=h.Aneg_b[:], func=AF.Exp))
        c.wait("v", "S", sv)
        vv = c.inc("v", lambda eng: eng.tensor_scalar(out=h.Aneg_b[:], in0=h.Aneg_b[:],
                                                      scalar1=-1.0, scalar2=None, op0=ALU.mult))
        c.wait("t", "V", vv)
    if "mamba_full" in need:
        c.wait("y", "T", c.cnt["T"])
        c.inc("y", lambda eng: eng.dma_start(out=h.row32b[:], in_=io["D_r"]))
        c.inc("y", lambda eng: eng.dma_start(out=h.rowDI[:], in_=io["norm_w_r"]))
        for e in ("t", "v"):
            c.wait(e, "ld", c.cnt["ld"])
        bc_row(h.row32b[:], 32, h.D_b[:])
        bc_row(h.rowDI[:], DI, h.normw_b[:])
    if "moe" in need:
        c.inc("y", lambda eng: eng.dma_start(out=h.rowI[:], in_=io["sb2_r"]))
        for e in ("t", "v"):
            c.wait(e, "ld", c.cnt["ld"])
        bc_row(h.rowI[:], INTER, h.sb2_b[:])
    if "l1" in need:
        c.inc("y", lambda eng: eng.dma_start(out=h.rowD[:], in_=io["bo_r"]))
        for e in ("t", "v"):
            c.wait(e, "ld", c.cnt["ld"])
        bc_row(h.rowD[:], D, h.bo_b[:])
        for src, dst in ((h.mur, h.omur), (h.muk, h.omuk)):
            c.inc("v", lambda eng, s=src, d=dst: eng.tensor_scalar(
                out=d[:], in0=s[:], scalar1=-1.0, scalar2=1.0, op0=ALU.mult, op1=ALU.add))
    c.barrier()
    return h


def emit_load_wchunks(c, es, io, name, rows, cols, tag=None, colseg=1096):
    nc = c.nc
    nch = rows // 128
    tag = tag or name
    dst = es.enter_context(nc.sbuf_tensor(f"w_{tag}", [128, nch * cols], BF16))
    segs = [(s, min(s + colseg, cols)) for s in range(0, cols, colseg)]
    with nc.sbuf_tensor(f"stg_{tag}", [128, segs[0][1]], F32, side="right") as stage:
        src = io[name].rearrange("(c p) d -> c p d", p=128)
        for ch in range(nch):
            for (s0, s1) in segs:
                c.wait("y", "G", c.cnt["G"])
                lv = c.inc("y", lambda eng, s=src[ch][:, s0:s1], d=stage[:, 0:s1 - s0]:
                           eng.dma_start(out=d, in_=s))
                c.wait("g", "ld", lv)
                c.inc("g", lambda eng, d=dst[:, ch * cols + s0: ch * cols + s1],
                      s=stage[:, 0:s1 - s0]: eng.tensor_copy(out=d, in_=s))
    gv = c.cnt["G"]
    for e in ("t", "v", "s"):
        c.wait(e, "G", gv)
    return dst


def emit_ln_rows(c, h, x_ap, nrows, scr, xn_dst_fn, dst_cast=True):
    m, sq, rstd, xn_tm = scr["m"], scr["sq"], scr["rstd"], scr["xn_tm"]
    R = nrows
    c.wait("s", "T", c.cnt["T"])
    c.inc("v", lambda eng: eng.reduce_sum(out=m[0:R, 0:1], in_=x_ap, axis=AX.X))
    sv = c.inc("s", lambda eng: eng.activation(out=sq[0:R, :], in_=x_ap, func=AF.Square,
                                               accum_out=m[0:R, 1:2]))
    c.wait("v", "S", sv)
    vv = c.incd("v", lambda eng: eng.tensor_scalar(out=m[0:R, 0:2], in0=m[0:R, 0:2],
                                                  scalar1=1.0 / D, scalar2=None, op0=ALU.mult))
    c.wait("s", "V", vv)
    sv = c.incd("s", lambda eng: eng.activation(out=rstd[0:R, 0:1], in_=m[0:R, 0:1],
                                               func=AF.Square))
    c.wait("v", "S", sv)
    vv = c.incd("v", lambda eng: eng.tensor_sub(out=rstd[0:R, 0:1], in0=m[0:R, 1:2],
                                               in1=rstd[0:R, 0:1]))
    c.wait("s", "V", vv)
    sv = c.incd("s", lambda eng: eng.activation(out=rstd[0:R, 0:1], in_=rstd[0:R, 0:1],
                                               func=AF.Sqrt, bias=h.kv[0:R, 0:1]))
    c.wait("v", "S", sv)
    c.inc("v", lambda eng: eng.reciprocal(out=rstd[0:R, 1:2], in_=rstd[0:R, 0:1]))
    vv = c.incd("v", lambda eng: eng.tensor_scalar(out=xn_tm[0:R, :], in0=x_ap,
                                                  scalar1=m[0:R, 0:1], scalar2=rstd[0:R, 1:2],
                                                  op0=ALU.subtract, op1=ALU.mult))
    c.wait("t", "V", vv)
    tv = None
    for j in range(8):
        tv = c.inc("t", lambda eng, i=xn_tm[0:R, j * 128:(j + 1) * 128],
                   o=h.psA[:, j * 128:j * 128 + R]:
                   eng.transpose(out=o, in_=i, identity=h.id32[0:R, 0:R]))
    c.wait("v", "T", tv)
    for j in range(8):
        c.inc("v", lambda eng, o=xn_dst_fn(j), i=h.psA[:, j * 128:j * 128 + R]:
              eng.tensor_copy(out=o, in_=i))
    vv = c.cnt["V"]
    c.wait("t", "V", vv)
    return vv


def _mk_scr(c, es):
    nc = c.nc
    scr = {
        "m": es.enter_context(nc.sbuf_tensor("scr_m", [128, 4], F32)),
        "sq": es.enter_context(nc.sbuf_tensor("scr_sq", [128, 1024], F32)),
        "rstd": es.enter_context(nc.sbuf_tensor("scr_rstd", [128, 2], F32)),
    }
    scr["xn_tm"] = scr["sq"]
    return scr


def emit_mamba(c, h, es, io, scr, full, dbg=False, dbg_tile=(0, 0)):
    MGS, MGT, MNG = 256, 2, 8
    nc = c.nc
    wcol0 = 0 if full else 2048
    wcols = RIN - wcol0
    W = emit_load_wchunks(c, es, io, "W_inT", 1024, RIN, tag="win") if full else \
        emit_load_wchunks_sub(c, es, io, "W_inT", 1024, RIN, wcol0, tag="win")
    if full:
        Wout = emit_load_wchunks(c, es, io, "W_outT", DI, 1024, tag="wout")
    if full:
        wcols = RIN

    xg = es.enter_context(nc.sbuf_tensor("m_xng", [128, 8 * MGS], BF16))
    xfm = es.enter_context(nc.sbuf_tensor("m_xfm", [128, 18 * MGS], BF16))
    slab = es.enter_context(nc.sbuf_tensor("m_slab", [128, MGS + 3], F32))
    tails = es.enter_context(nc.sbuf_tensor("m_tails", [128, 54], F32))
    xtile2h = es.enter_context(nc.sbuf_tensor("m_xtile", [128, 2048], F32))
    xtile2 = [xtile2h[:, 0:1024], xtile2h[:, 1024:2048]]
    xtile = xtile2[0]
    dtv = es.enter_context(nc.sbuf_tensor("m_dtv", [128, 32], F32))
    dts1 = es.enter_context(nc.sbuf_tensor("m_dts1", [128, 32], F32))
    dtT = es.enter_context(nc.sbuf_tensor("m_dtT", [128, 32], F32))
    a_T = es.enter_context(nc.sbuf_tensor("m_aT", [128, 32], F32))
    AcumT = es.enter_context(nc.sbuf_tensor("m_AcumT", [128, 32], F32))
    Trow = es.enter_context(nc.sbuf_tensor("m_Trow", [1, 32], F32))
    w_T = es.enter_context(nc.sbuf_tensor("m_wT", [128, 32], F32))
    eT_b = es.enter_context(nc.sbuf_tensor("m_eTbc", [128, 32], F32))
    dtw = es.enter_context(nc.sbuf_tensor("m_dtw", [128, 32], F32))
    xdw = es.enter_context(nc.sbuf_tensor("m_xdw", [128, DI], BF16))
    B_tm = es.enter_context(nc.sbuf_tensor("m_Btm", [128, 128], BF16))
    S_a = es.enter_context(nc.sbuf_tensor("m_Sa", [128, DI], F32))
    if full:
        u_T = es.enter_context(nc.sbuf_tensor("m_uT", [128, 32], F32))
        xd = es.enter_context(nc.sbuf_tensor("m_xd", [128, DI], BF16))
        amq = es.enter_context(nc.sbuf_tensor("m_amq", [128, 1024], F32))
        Mtq = es.enter_context(nc.sbuf_tensor("m_Mtq", [128, 1024], BF16))
        CBm = es.enter_context(nc.sbuf_tensor("m_CBm", [128, 128], BF16))
        S_bf = es.enter_context(nc.sbuf_tensor("m_Sbf", [128, DI], BF16))
        siluz = es.enter_context(nc.sbuf_tensor("m_siluz", [128, DI], BF16))
        t1 = es.enter_context(nc.sbuf_tensor("m_t1", [128, DI], F32))
        t3 = es.enter_context(nc.sbuf_tensor("m_t3", [128, DI], BF16))
        yn = es.enter_context(nc.sbuf_tensor("m_yn", [128, DI], BF16))
        y_fm = xdw
        xout = scr["sq"]
        S_b = t1
    else:
        S_b = es.enter_context(nc.sbuf_tensor("m_Sb", [128, DI], F32))

    if full:
        lv = c.inc("y", lambda eng: eng.dma_start(out=S_a[:], in_=io["S_in"]))
        c.wait("v", "ld", lv)
        vv = c.incd("v", lambda eng: eng.tensor_copy(out=S_bf[:], in_=S_a[:]))
        c.wait("t", "V", vv)
    else:
        vv = c.inc("v", lambda eng: eng.memset(S_a[:], 0.0))
        c.wait("t", "V", vv)

    es_halo = contextlib.ExitStack()
    uhn128 = es_halo.enter_context(nc.sbuf_tensor("m_uhn128", [128, 1024], BF16, side="right"))
    vv = c.inc("v", lambda eng: eng.memset(xtile, 0.0))
    c.wait("y", "V", vv)
    lv = c.inc("y", lambda eng: eng.dma_start(out=xtile2h[0:3, 0:1024], in_=io["uh"]))
    c.wait("v", "ld", lv)
    c.wait("s", "ld", lv)
    emit_ln_rows(c, h, xtile, 128, scr,
                 lambda j: uhn128[:, j * 128:(j + 1) * 128])
    for rc in range(18):
        wc = (2048 - wcol0) + rc * 128
        for dc in range(8):
            c.inc("t", lambda eng, l=W[:, dc * wcols + wc: dc * wcols + wc + 128],
                  r=uhn128[:, dc * 128: dc * 128 + 3], o=h.psB[:, rc * 4: rc * 4 + 3],
                  st=(dc == 0), sp=(dc == 7):
                  eng.matmul(out=o, lhsT=l, rhs=r, start=st, stop=sp))
    tv = c.cnt["T"]
    c.wait("v", "T", tv)
    vv = c.inc("v", lambda eng: eng.tensor_copy(
        out=tails[:].rearrange("p (r k) -> p r k", k=3),
        in_=h.psB[:, 0:72].rearrange("p (r k) -> p r k", k=4)[:, :, 0:3]))
    c.wait("t", "V", vv)
    es_halo.close()

    u_t = io["u"].rearrange("(n p) d -> n p d", p=128)
    x1_t = io["x1"].rearrange("(n p) d -> n p d", p=128) if full else None

    for g in range(MNG):
        for k in range(MGT):
            ti = g * MGT + k
            buf = xtile2[k % 2]
            c.wait("y", "V", c.cnt["V"])
            c.wait("y", "S", c.cnt["S"])
            c.wait("s", "st", c.cnt["st"])
            lv = c.inc("y", lambda eng, s=u_t[ti], d=buf: eng.dma_start(out=d, in_=s))
            c.wait("v", "ld", lv)
            c.wait("s", "ld", lv)
            emit_ln_rows(c, h, buf, 128, scr,
                         lambda j, k=k: xg[:, j * MGS + k * 128: j * MGS + (k + 1) * 128])
        for rc in range(18):
            wc = (2048 - wcol0) + rc * 128
            c.wait("t", "V", c.cnt["V"])
            for dc in range(8):
                c.inc("t", lambda eng, l=W[:, dc * wcols + wc: dc * wcols + wc + 128],
                      r=xg[:, dc * MGS:(dc + 1) * MGS], o=h.psB[:, 0:MGS],
                      st=(dc == 0), sp=(dc == 7):
                      eng.matmul(out=o, lhsT=l, rhs=r, start=st, stop=sp))
            tv = c.cnt["T"]
            c.wait("v", "T", tv)
            c.inc("v", lambda eng, rc=rc: eng.tensor_copy(out=slab[:, 0:3],
                                                          in_=tails[:, rc * 3:(rc + 1) * 3]))
            vv = c.inc("v", lambda eng: eng.tensor_copy(out=slab[:, 3:3 + MGS], in_=h.psB[:, 0:MGS]))
            c.wait("t", "V", vv)
            c.inc("v", lambda eng, rc=rc: eng.tensor_copy(out=tails[:, rc * 3:(rc + 1) * 3],
                                                          in_=slab[:, MGS:MGS + 3]))
            c.inc("v", lambda eng, rc=rc: eng.tensor_scalar(
                out=scr["sq"][:, 0:MGS], in0=slab[:, 0:MGS],
                scalar1=h.convw[:, rc * 4:rc * 4 + 1], scalar2=None, op0=ALU.mult))
            for kk in range(1, 4):
                c.inc("v", lambda eng, rc=rc, kk=kk: eng.scalar_tensor_tensor(
                    out=scr["sq"][:, 0:MGS], in0=slab[:, kk:kk + MGS],
                    scalar=h.convw[:, rc * 4 + kk:rc * 4 + kk + 1], in1=scr["sq"][:, 0:MGS],
                    op0=ALU.mult, op1=ALU.add))
            vv = c.cnt["V"]
            c.wait("s", "V", vv)
            sv = c.inc("s", lambda eng, rc=rc: eng.activation(
                out=xfm[:, rc * MGS:(rc + 1) * MGS], in_=scr["sq"][:, 0:MGS], func=AF.Silu,
                bias=h.convb[:, rc:rc + 1]))
            c.wait("v", "S", sv)

        svx = c.cnt["S"]
        c.wait("t", "S", svx)

        for k in range(MGT):
            ti = g * MGT + k
            tcol = k * 128

            wc = 4352 - wcol0
            c.wait("t", "V", c.cnt["V"])
            for dc in range(8):
                c.inc("t", lambda eng, l=xg[:, dc * MGS + tcol: dc * MGS + tcol + 128],
                      r=W[:, dc * wcols + wc: dc * wcols + wc + 32], o=h.psB[:, 0:32],
                      st=(dc == 0), sp=(dc == 7):
                      eng.matmul(out=o, lhsT=l, rhs=r, start=st, stop=sp))
            tv = c.cnt["T"]
            c.wait("v", "T", tv)
            vv = c.incd("v", lambda eng: eng.tensor_add(out=dtv[:], in0=h.psB[:, 0:32],
                                                       in1=h.dtb_b[:]))
            c.wait("s", "V", vv)
            c.inc("s", lambda eng: eng.activation(out=dts1[:], in_=dtv[:], func=AF.Relu))
            c.inc("s", lambda eng: eng.activation(out=dtT[:], in_=dtv[:], func=AF.Abs))
            c.inc("s", lambda eng: eng.activation(out=dtT[:], in_=dtT[:], func=AF.Exp, scale=-1.0))
            sv = c.incd("s", lambda eng: eng.activation(out=dtT[:], in_=dtT[:], func=AF.Ln, bias=h.kv[:, 1:2]))
            c.wait("v", "S", sv)
            c.inc("v", lambda eng: eng.tensor_add(out=dtT[:], in0=dts1[:], in1=dtT[:]))
            vv = c.incd("v", lambda eng: eng.tensor_mul(out=a_T[:], in0=dtT[:], in1=h.Aneg_b[:]))
            c.wait("t", "V", vv)

            c.inc("t", lambda eng: eng.matmul(out=h.psB[:, 64:96], lhsT=h.U128[:], rhs=a_T[:],
                                              start=True, stop=True))
            tvT = c.inc("t", lambda eng: eng.matmul(out=h.psB[0:1, 128:160], lhsT=h.ones_col[:],
                                                    rhs=a_T[:], start=True, stop=True))
            c.wait("v", "T", tvT)
            c.wait("s", "T", tvT)
            vv = c.incd("v", lambda eng: eng.tensor_copy(out=AcumT[:], in_=h.psB[:, 64:96]))
            if full:
                c.incd("s", lambda eng: eng.activation(out=u_T[:], in_=h.psB[:, 64:96], func=AF.Exp))
            sv = c.incd("s", lambda eng: eng.copy(out=Trow[:], in_=h.psB[0:1, 128:160]))
            c.wait("t", "S", sv)
            c.wait("t", "V", vv)
            tvB = c.inc("t", lambda eng: eng.matmul(out=h.psB[:, 192:224], lhsT=h.ones_row[:],
                                                    rhs=Trow[:], start=True, stop=True))
            c.wait("v", "T", tvB)
            c.wait("s", "T", tvB)
            vv = c.incd("v", lambda eng: eng.tensor_sub(out=w_T[:], in0=h.psB[:, 192:224],
                                                       in1=AcumT[:]))
            sv1 = c.incd("s", lambda eng: eng.activation(out=eT_b[:], in_=h.psB[:, 192:224],
                                                        func=AF.Exp))
            c.wait("s", "V", vv)
            sv = c.incd("s", lambda eng: eng.activation(out=w_T[:], in_=w_T[:], func=AF.Exp))
            c.wait("v", "S", sv)
            c.wait("t", "S", sv)
            c.inc("v", lambda eng: eng.tensor_mul(out=dtw[:], in0=dtT[:], in1=w_T[:]))

            for rnd in range(2):
                c.wait("t", "V", c.cnt["V"])
                for jj in range(8):
                    j = rnd * 8 + jj
                    c.inc("t", lambda eng, i=xfm[:, j * MGS + tcol: j * MGS + tcol + 128],
                          o=h.psTb[:, jj * 128:(jj + 1) * 128]:
                          eng.transpose(out=o, in_=i, identity=h.idbf[:]))
                tv = c.cnt["T"]
                c.wait("v", "T", tv)
                ps3 = h.psTb[:, 0:1024].rearrange("p (hh q) -> p hh q", q=64)
                vv = c.incd("v", lambda eng, rnd=rnd, ps3=ps3: eng.tensor_tensor(
                    out=xdw[:, rnd * 1024:(rnd + 1) * 1024].rearrange(
                        "p (hh q) -> p hh q", q=64),
                    in0=ps3, in1=bcast_ap(dtw[:, rnd * 16:(rnd + 1) * 16], 16, 64),
                    op=ALU.mult))
                if full:
                    vv = c.incd("v", lambda eng, rnd=rnd, ps3=ps3: eng.tensor_tensor(
                        out=xd[:, rnd * 1024:(rnd + 1) * 1024].rearrange(
                            "p (hh q) -> p hh q", q=64),
                        in0=ps3, in1=bcast_ap(dtT[:, rnd * 16:(rnd + 1) * 16], 16, 64),
                        op=ALU.mult))
                c.wait("t", "V", vv)

            tv = c.inc("t", lambda eng, i=xfm[:, 16 * MGS + tcol: 16 * MGS + tcol + 128]:
                       eng.transpose(out=h.psTb[:, 0:128], in_=i, identity=h.idbf[:]))
            c.wait("v", "T", tv)
            vv = c.incd("v", lambda eng: eng.tensor_copy(out=B_tm[:], in_=h.psTb[:, 0:128]))
            c.wait("t", "V", vv)

            if full:
                tv = c.inc("t", lambda eng,
                           l=xfm[:, 16 * MGS + tcol: 16 * MGS + tcol + 128],
                           r=xfm[:, 17 * MGS + tcol: 17 * MGS + tcol + 128]:
                           eng.matmul(out=h.psB[:, 0:128], lhsT=l, rhs=r,
                                      start=True, stop=True))
                c.wait("v", "T", tv)
                vv = c.incd("v", lambda eng: eng.tensor_mul(out=CBm[:], in0=h.psB[:, 0:128],
                                                           in1=h.TRIU32[:]))
                c.wait("t", "V", vv)

                for seg in range(4):
                    wc = seg * 512 - wcol0
                    for dc in range(8):
                        c.inc("t", lambda eng, l=xg[:, dc * MGS + tcol: dc * MGS + tcol + 128],
                              r=W[:, dc * wcols + wc: dc * wcols + wc + 512],
                              o=h.psB[:, seg * 512:(seg + 1) * 512],
                              st=(dc == 0), sp=(dc == 7):
                              eng.matmul(out=o, lhsT=l, rhs=r, start=st, stop=sp))
                tv = c.cnt["T"]
                c.wait("s", "T", tv)
                sv = c.inc("s", lambda eng: eng.activation(out=siluz[:], in_=h.psB[:, 0:DI],
                                                           func=AF.Silu))
                c.wait("t", "S", sv)
                c.wait("v", "S", sv)

                for hh in range(32):
                    c.inc("t", lambda eng, hh=hh,
                          l=xfm[:, 17 * MGS + tcol: 17 * MGS + tcol + 128]: eng.matmul(
                        out=h.psB[:, hh * 64:(hh + 1) * 64], lhsT=l,
                        rhs=S_bf[:, hh * 64:(hh + 1) * 64], start=True, stop=True))
                tv = c.cnt["T"]
                c.wait("v", "T", tv)
                vv = c.inc("v", lambda eng: eng.tensor_tensor(
                    out=t1[:].rearrange("p (hh q) -> p hh q", q=64),
                    in0=h.psB[:, 0:DI].rearrange("p (hh q) -> p hh q", q=64),
                    in1=bcast_ap(u_T[:], 32, 64), op=ALU.mult))
                c.wait("t", "V", vv)
                for qq in range(4):
                    h0 = qq * 8
                    vv = c.incd("v", lambda eng, h0=h0: eng.tensor_tensor(
                        out=amq[:].rearrange("p (hh l) -> p hh l", l=128),
                        in0=bcast_ap(a_T[:, h0:h0 + 8], 8, 128),
                        in1=h.TRIU32[:].unsqueeze(1).broadcast_to([128, 8, 128]),
                        op=ALU.mult))
                    c.wait("t", "V", vv)
                    for q2 in range(2):
                        c.inc("t", lambda eng, q2=q2: eng.matmul(
                            out=h.psA[:, q2 * 512:(q2 + 1) * 512], lhsT=h.SLT32[:],
                            rhs=amq[:, q2 * 512:(q2 + 1) * 512], start=True, stop=True))
                    tv = c.cnt["T"]
                    c.wait("s", "T", tv)
                    sv = c.inc("s", lambda eng: eng.activation(out=Mtq[:], in_=h.psA[:, 0:1024],
                                                               func=AF.Exp))
                    c.wait("v", "S", sv)
                    c.wait("t", "S", sv)
                    vv = c.incd("v", lambda eng: eng.tensor_tensor(
                        out=Mtq[:].rearrange("p (hh l) -> p hh l", l=128),
                        in0=Mtq[:].rearrange("p (hh l) -> p hh l", l=128),
                        in1=CBm[:].unsqueeze(1).broadcast_to([128, 8, 128]), op=ALU.mult))
                    c.wait("t", "V", vv)
                    for hh in range(h0, h0 + 8):
                        c.inc("t", lambda eng, hh=hh, h0=h0: eng.matmul(
                            out=h.psB[:, hh * 64:(hh + 1) * 64],
                            lhsT=Mtq[:, (hh - h0) * 128:(hh - h0 + 1) * 128],
                            rhs=xd[:, hh * 64:(hh + 1) * 64], start=True, stop=True))
                tv = c.cnt["T"]
                c.inc("v", lambda eng: eng.reciprocal(out=dts1[:], in_=dtT[:]))
                c.inc("v", lambda eng: eng.tensor_mul(out=dts1[:], in0=dts1[:], in1=h.D_b[:]))
                c.inc("v", lambda eng: eng.tensor_tensor(
                    out=t3[:].rearrange("p (hh q) -> p hh q", q=64),
                    in0=xd[:].rearrange("p (hh q) -> p hh q", q=64),
                    in1=bcast_ap(dts1[:], 32, 64), op=ALU.mult))
                c.inc("v", lambda eng: eng.tensor_add(out=t1[:], in0=t1[:], in1=t3[:]))
                c.wait("v", "T", tv)
                vv = c.inc("v", lambda eng: eng.tensor_add(out=t1[:], in0=h.psB[:, 0:DI],
                                                           in1=t1[:]))
                c.wait("t", "V", vv)

            for hh in range(32):
                c.inc("t", lambda eng, hh=hh: eng.matmul(
                    out=h.psB[:, hh * 64:(hh + 1) * 64], lhsT=B_tm[:],
                    rhs=xdw[:, hh * 64:(hh + 1) * 64], start=True, stop=True))
            tvG = c.cnt["T"]

            if full:
                vv = c.inc("v", lambda eng: eng.tensor_mul(out=t1[:], in0=t1[:], in1=siluz[:]))
                c.wait("s", "V", vv)
                sv = c.inc("s", lambda eng: eng.activation(out=t3[:], in_=t1[:], func=AF.Square,
                                                           accum_out=scr["m"][:, 2:3]))
                c.inc("s", lambda eng: eng.activation(out=scr["m"][:, 3:4], in_=scr["m"][:, 2:3],
                                                      func=AF.Ln, bias=h.kv[:, 0:1], scale=1.0 / DI))
                sv = c.incd("s", lambda eng: eng.activation(out=scr["m"][:, 3:4], in_=scr["m"][:, 3:4],
                                                           func=AF.Exp, scale=-0.5))
                c.wait("v", "S", sv)
                vv = c.incd("v", lambda eng: eng.scalar_tensor_tensor(
                    out=yn[:], in0=t1[:], scalar=scr["m"][:, 3:4], in1=h.normw_b[:],
                    op0=ALU.mult, op1=ALU.mult))

            c.wait("v", "T", tvG)
            c.inc("v", lambda eng: eng.tensor_tensor(
                out=S_b[:].rearrange("p (hh q) -> p hh q", q=64),
                in0=S_a[:].rearrange("p (hh q) -> p hh q", q=64),
                in1=bcast_ap(eT_b[:], 32, 64), op=ALU.mult))
            vv = c.inc("v", lambda eng: eng.tensor_add(out=S_a[:], in0=S_b[:], in1=h.psB[:, 0:DI]))
            c.wait("t", "V", vv)
            if dbg and (g, k) == dbg_tile:
                c.barrier()
                dlist = [("dbg_tails", tails),
                         ("dbg_xg", xg), ("dbg_xfm", xfm), ("dbg_dtT", dtT),
                         ("dbg_aT", a_T), ("dbg_AcumT", AcumT), ("dbg_wT", w_T),
                         ("dbg_eTb", eT_b), ("dbg_xdw", xdw), ("dbg_Btm", B_tm),
                         ("dbg_S", S_a)]
                if full:
                    dlist += [("dbg_xd", xd), ("dbg_uT", u_T), ("dbg_siluz", siluz),
                              ("dbg_t1", t1), ("dbg_yn", yn), ("dbg_CBm", CBm),
                              ("dbg_Db", h.D_b), ("dbg_normwb", h.normw_b),
                              ("dbg_rms", scr["m"]), ("dbg_Mtq", Mtq), ("dbg_amq", amq)]
                for nm, buf in dlist:
                    c.inc("g", lambda eng, d=io[nm], s=buf[:]: eng.dma_start(out=d, in_=s),
                          sname="st")
                c.barrier()
            if full:
                vv = c.incd("v", lambda eng: eng.tensor_copy(out=S_bf[:], in_=S_a[:]))

            if full:
                for rnd in range(2):
                    c.wait("t", "V", c.cnt["V"])
                    for jj in range(8):
                        j = rnd * 8 + jj
                        c.inc("t", lambda eng, i=yn[:, j * 128:(j + 1) * 128],
                              o=h.psTb[:, jj * 128:(jj + 1) * 128]:
                              eng.transpose(out=o, in_=i, identity=h.idbf[:]))
                    tv = c.cnt["T"]
                    c.wait("v", "T", tv)
                    c.inc("v", lambda eng, rnd=rnd: eng.tensor_copy(
                        out=y_fm[:, rnd * 1024:(rnd + 1) * 1024], in_=h.psTb[:, 0:1024]))
                vv = c.cnt["V"]
                c.wait("t", "V", vv)
                lv = c.inc("y", lambda eng, s=u_t[ti], d=xtile2[k % 2]:
                           eng.dma_start(out=d, in_=s))
                for seg in range(2):
                    for kc in range(16):
                        c.inc("t", lambda eng, kc=kc, seg=seg, st=(kc == 0), sp=(kc == 15):
                              eng.matmul(
                                  out=h.psB[:, seg * 512:(seg + 1) * 512],
                                  lhsT=y_fm[:, kc * 128:(kc + 1) * 128],
                                  rhs=Wout[:, kc * 1024 + seg * 512: kc * 1024 + (seg + 1) * 512],
                                  start=st, stop=sp))
                tv = c.cnt["T"]
                c.wait("v", "T", tv)
                c.wait("v", "ld", lv)
                vv = c.incd("v", lambda eng, i0=xtile2[k % 2]: eng.tensor_add(
                    out=xout[:, 0:1024], in0=i0, in1=h.psB[:, 0:1024]))
                c.wait("t", "V", vv)
                c.wait("g", "V", vv)
                stv = c.inc("g", lambda eng, d=x1_t[ti]: eng.dma_start(
                    out=d, in_=xout[:, 0:1024]), sname="st")
                c.wait("v", "st", stv)
                c.wait("y", "st", stv)

    if not full:
        c.barrier()
        stv = c.inc("g", lambda eng: eng.dma_start(out=io["S_out"], in_=S_a[:]), sname="st")
        c.wait("g", "st", stv)


def emit_load_wchunks_sub(c, es, io, name, rows, cols, col0, tag=None, colseg=1096):
    nc = c.nc
    nch = rows // 128
    ncols = cols - col0
    tag = tag or name
    dst = es.enter_context(nc.sbuf_tensor(f"w_{tag}", [128, nch * ncols], BF16))
    segs = [(s, min(s + colseg, ncols)) for s in range(0, ncols, colseg)]
    with nc.sbuf_tensor(f"stg_{tag}", [128, segs[0][1]], F32, side="right") as stage:
        src = io[name].rearrange("(ch p) d -> ch p d", p=128)
        for ch in range(nch):
            for (s0, s1) in segs:
                c.wait("y", "G", c.cnt["G"])
                lv = c.inc("y", lambda eng, s=src[ch][:, col0 + s0: col0 + s1],
                           d=stage[:, 0:s1 - s0]: eng.dma_start(out=d, in_=s))
                c.wait("g", "ld", lv)
                c.inc("g", lambda eng, d=dst[:, ch * ncols + s0: ch * ncols + s1],
                      s=stage[:, 0:s1 - s0]: eng.tensor_copy(out=d, in_=s))
    gv = c.cnt["G"]
    for e in ("t", "v", "s"):
        c.wait(e, "G", gv)
    return dst


def emit_moe(c, h, es, io, scr):
    nc = c.nc
    xg = es.enter_context(nc.sbuf_tensor("e_xng", [128, 8 * GS], BF16))
    xtile = es.enter_context(nc.sbuf_tensor("e_xtile", [128, 1024], F32))
    gwT = es.enter_context(nc.sbuf_tensor("e_gwT", [128, 64], BF16))
    gwst = es.enter_context(nc.sbuf_tensor("e_gwst", [128, 64], F32, side="right"))
    b1f = es.enter_context(nc.sbuf_tensor("e_b1f", [128, 72], F32))
    b3f = es.enter_context(nc.sbuf_tensor("e_b3f", [128, 72], F32))
    b2sb = es.enter_context(nc.sbuf_tensor("e_b2sb", [8, 1024], BF16))
    b2st = es.enter_context(nc.sbuf_tensor("e_b2st", [8, 1024], F32, side="right"))
    probs = es.enter_context(nc.sbuf_tensor("e_probs", [128, 8], F32))
    ltb = es.enter_context(nc.sbuf_tensor("e_ltb", [128, 8], F32))
    rm = es.enter_context(nc.sbuf_tensor("e_rm", [128, 4], F32))
    comb = es.enter_context(nc.sbuf_tensor("e_comb", [128, 4 * 32], F32))
    combT = es.enter_context(nc.sbuf_tensor("e_combT", [32, 4 * 128], BF16))
    h1s = es.enter_context(nc.sbuf_tensor("e_h1s", [128, 8 * GS], BF16))
    h3b = es.enter_context(nc.sbuf_tensor("e_h3b", [128, 8 * GS], BF16))
    y_acc = es.enter_context(nc.sbuf_tensor("e_yacc", [128, 4 * 1024], F32))
    Wbuf = [es.enter_context(nc.sbuf_tensor(f"e_W{b}", [128, 3 * 8192], BF16)) for b in range(2)]
    stage = es.enter_context(nc.sbuf_tensor("e_stg", [128, 4096], F32, side="right"))

    lv = c.inc("y", lambda eng: eng.dma_start(
        out=b1f[:, 0:64].rearrange("p (e i) -> p e i", i=8),
        in_=io["moe_b1_fm"].rearrange("e p i -> p e i")))
    lv = c.inc("y", lambda eng: eng.dma_start(
        out=b3f[:, 0:64].rearrange("p (e i) -> p e i", i=8),
        in_=io["moe_b3_fm"].rearrange("e p i -> p e i")))
    lv = c.inc("y", lambda eng: eng.dma_start(out=b1f[:, 64:72], in_=io["sb1_fm"]))
    lv = c.inc("y", lambda eng: eng.dma_start(out=b3f[:, 64:72], in_=io["sb3_fm"]))
    lv = c.inc("y", lambda eng: eng.dma_start(out=b2st[:], in_=io["moe_b2"]))
    lv = c.inc("y", lambda eng: eng.dma_start(
        out=gwst[:].rearrange("p (ch i) -> p ch i", i=8),
        in_=io["gate_wT"].rearrange("(ch p) i -> p ch i", p=128)))
    c.wait("g", "ld", lv)
    c.inc("g", lambda eng: eng.tensor_copy(out=b2sb[:], in_=b2st[:]))
    gv = c.inc("g", lambda eng: eng.tensor_copy(out=gwT[:], in_=gwst[:]))
    for e in ("t", "v", "s"):
        c.wait(e, "G", gv)
    vv = c.inc("v", lambda eng: eng.memset(comb[:], 0.0))
    vv = c.inc("v", lambda eng: eng.memset(combT[:], 0.0))
    c.wait("t", "V", vv)

    def w_names(e):
        if e < NE:
            return [("moe_w1", e), ("moe_w3", e), ("moe_w2", e)]
        return [("sw1", None), ("sw3", None), ("sw2", None)]

    def load_expert(e, guardT):
        buf = Wbuf[e % 2]
        for mi, (nm, ei) in enumerate(w_names(e)):
            src = io[nm]
            if ei is not None:
                src = src[ei]
            src = src.rearrange("(ch p) d -> p ch d", p=128)
            for hf in range(2):
                c.wait("y", "G", c.cnt["G"])
                c.wait("y", "T", guardT)
                lv = c.inc("y", lambda eng, s=src[:, hf * 4:(hf + 1) * 4, :]: eng.dma_start(
                    out=stage[:].rearrange("p (ch d) -> p ch d", d=1024), in_=s))
                c.wait("g", "ld", lv)
                c.inc("g", lambda eng, d=buf[:, mi * 8192 + hf * 4096:
                                             mi * 8192 + (hf + 1) * 4096]:
                      eng.tensor_copy(out=d, in_=stage[:]))
        return c.cnt["G"]

    u_t = io["u"].rearrange("(n p) d -> n p d", p=128)
    x1_t = io["x1"].rearrange("(n p) d -> n p d", p=128)
    wready = [0, 0]
    wguardT = [0, 0]
    wready[0] = load_expert(0, 0)

    for g in range(NG):
        for k in range(4):
            ti = g * 4 + k
            c.wait("y", "V", c.cnt["V"])
            c.wait("y", "S", c.cnt["S"])
            lv = c.inc("y", lambda eng, s=u_t[ti]: eng.dma_start(out=xtile[:], in_=s))
            c.wait("v", "ld", lv)
            c.wait("s", "ld", lv)
            emit_ln_rows(c, h, xtile[:], 128, scr,
                         lambda j, k=k: xg[:, j * GS + k * 128: j * GS + (k + 1) * 128])
        for k in range(4):
            c.wait("t", "V", c.cnt["V"])
            for dc in range(8):
                c.inc("t", lambda eng, l=xg[:, dc * GS + k * 128: dc * GS + (k + 1) * 128],
                      r=gwT[:, dc * 8:(dc + 1) * 8], o=h.psB[:, 0:8],
                      st=(dc == 0), sp=(dc == 7):
                      eng.matmul(out=o, lhsT=l, rhs=r, start=st, stop=sp))
            tv = c.cnt["T"]
            c.wait("v", "T", tv)
            c.inc("v", lambda eng: eng.reduce_max(out=rm[:, 0:1], in_=h.psB[:, 0:8], axis=AX.X))
            vv = c.inc("v", lambda eng: eng.tensor_scalar(out=rm[:, 1:2], in0=rm[:, 0:1],
                                                          scalar1=-1.0, scalar2=None, op0=ALU.mult))
            c.wait("s", "V", vv)
            sv = c.inc("s", lambda eng: eng.activation(out=probs[:], in_=h.psB[:, 0:8],
                                                       func=AF.Exp, bias=rm[:, 1:2]))
            c.wait("v", "S", sv)
            c.wait("t", "S", sv)
            c.inc("v", lambda eng: eng.reduce_sum(out=rm[:, 2:3], in_=probs[:], axis=AX.X))
            c.inc("v", lambda eng: eng.reciprocal(out=rm[:, 3:4], in_=rm[:, 2:3]))
            c.inc("v", lambda eng: eng.tensor_scalar(out=probs[:], in0=probs[:],
                                                     scalar1=rm[:, 3:4], scalar2=None,
                                                     op0=ALU.mult))
            c.inc("v", lambda eng: eng.reduce_max(out=rm[:, 0:1], in_=probs[:], axis=AX.X))
            c.inc("v", lambda eng: eng.tensor_scalar(out=ltb[:], in0=probs[:], scalar1=rm[:, 0:1],
                                                     scalar2=None, op0=ALU.is_lt))
            c.inc("v", lambda eng: eng.tensor_scalar(out=ltb[:], in0=ltb[:], scalar1=1000.0,
                                                     scalar2=-1000.0, op0=ALU.mult, op1=ALU.add))
            c.inc("v", lambda eng: eng.tensor_add(out=ltb[:], in0=ltb[:], in1=probs[:]))
            c.inc("v", lambda eng: eng.reduce_max(out=rm[:, 1:2], in_=ltb[:], axis=AX.X))
            c.inc("v", lambda eng: eng.tensor_scalar(out=ltb[:], in0=probs[:], scalar1=rm[:, 1:2],
                                                     scalar2=None, op0=ALU.is_lt))
            c.inc("v", lambda eng: eng.tensor_scalar(out=ltb[:], in0=ltb[:], scalar1=-1.0,
                                                     scalar2=1.0, op0=ALU.mult, op1=ALU.add))
            vv = c.inc("v", lambda eng, k=k: eng.tensor_mul(out=comb[:, k * 32:k * 32 + 8],
                                                            in0=probs[:], in1=ltb[:]))
            c.wait("t", "V", vv)
            tv = c.inc("t", lambda eng, k=k: eng.transpose(
                out=h.psA[0:32, 0:128], in_=comb[:, k * 32:(k + 1) * 32], identity=h.id32[:]))
            c.wait("v", "T", tv)
            vv = c.inc("v", lambda eng, k=k: eng.tensor_copy(
                out=combT[:, k * 128:(k + 1) * 128], in_=h.psA[0:32, 0:128]))
            c.wait("t", "V", vv)

        vv = c.inc("v", lambda eng: eng.memset(y_acc[:], 0.0))
        c.wait("t", "V", vv)
        c.wait("t", "S", c.cnt["S"])

        for e in range(NE + 1):
            gv = wready[e % 2]
            for en in ("t",):
                c.wait(en, "G", gv)
            buf = Wbuf[e % 2]
            bcol = 64 if e == NE else e * 8
            s_base = c.cnt["S"]
            v_base = None
            for m_i, (dstbuf, func, bias) in enumerate(
                    ((h1s, AF.Silu, b1f), (h3b, AF.Copy, b3f))):
                base = m_i * 8192
                if m_i == 1:
                    v_base = c.cnt["V"]
                for ic in range(8):
                    gidx = m_i * 8 + ic
                    seg = gidx % 4
                    if gidx >= 4:
                        if gidx - 4 < 8:
                            c.wait("t", "S", s_base + (gidx - 4) + 1)
                        else:
                            c.wait("t", "V", v_base + (gidx - 4 - 8) + 1)
                    for dc in range(8):
                        c.inc("t", lambda eng, l=buf[:, base + dc * 1024 + ic * 128:
                                                     base + dc * 1024 + ic * 128 + 128],
                              r=xg[:, dc * GS:(dc + 1) * GS],
                              o=h.psB[:, seg * 512:(seg + 1) * 512],
                              st=(dc == 0), sp=(dc == 7):
                              eng.matmul(out=o, lhsT=l, rhs=r, start=st, stop=sp))
                    tv = c.cnt["T"]
                    if func is AF.Silu:
                        c.wait("s", "T", tv)
                        c.inc("s", lambda eng, ic=ic, dstbuf=dstbuf, bias=bias,
                              seg=seg, bcol=bcol:
                              eng.activation(out=dstbuf[:, ic * GS:(ic + 1) * GS],
                                             in_=h.psB[:, seg * 512:(seg + 1) * 512],
                                             func=AF.Silu,
                                             bias=bias[:, bcol + ic:bcol + ic + 1]))
                    else:
                        c.wait("v", "T", tv)
                        c.inc("v", lambda eng, ic=ic, dstbuf=dstbuf, bias=bias,
                              seg=seg, bcol=bcol:
                              eng.tensor_scalar(out=dstbuf[:, ic * GS:(ic + 1) * GS],
                                                in0=h.psB[:, seg * 512:(seg + 1) * 512],
                                                scalar1=bias[:, bcol + ic:bcol + ic + 1],
                                                scalar2=None, op0=ALU.add))
            sv = c.cnt["S"]
            c.wait("t", "S", sv)
            c.wait("v", "S", sv)
            for ic in range(8):
                c.inc("v", lambda eng, ic=ic: eng.tensor_mul(
                    out=h1s[:, ic * GS:(ic + 1) * GS], in0=h1s[:, ic * GS:(ic + 1) * GS],
                    in1=h3b[:, ic * GS:(ic + 1) * GS]))
            vv = c.cnt["V"]
            c.wait("t", "V", vv)
            for k in range(4):
                for seg in range(2):
                    for ic in range(8):
                        c.inc("t", lambda eng, ic=ic, k=k, seg=seg, buf=buf,
                              st=(ic == 0), sp=(ic == 7):
                              eng.matmul(
                                  out=h.psB[:, (k % 2) * 1024 + seg * 512:
                                            (k % 2) * 1024 + (seg + 1) * 512],
                                  lhsT=h1s[:, ic * GS + k * 128: ic * GS + (k + 1) * 128],
                                  rhs=buf[:, 16384 + ic * 1024 + seg * 512:
                                          16384 + ic * 1024 + (seg + 1) * 512],
                                  start=st, stop=sp))
                tv = c.cnt["T"]
                c.wait("v", "T", tv)
                for seg in range(2):
                    if e < NE:
                        c.inc("v", lambda eng, k=k, seg=seg, e=e: eng.scalar_tensor_tensor(
                            out=y_acc[:, k * 1024 + seg * 512: k * 1024 + (seg + 1) * 512],
                            in0=h.psB[:, (k % 2) * 1024 + seg * 512:
                                      (k % 2) * 1024 + (seg + 1) * 512],
                            scalar=comb[:, k * 32 + e: k * 32 + e + 1],
                            in1=y_acc[:, k * 1024 + seg * 512: k * 1024 + (seg + 1) * 512],
                            op0=ALU.mult, op1=ALU.add))
                    else:
                        c.inc("v", lambda eng, k=k, seg=seg: eng.tensor_add(
                            out=y_acc[:, k * 1024 + seg * 512: k * 1024 + (seg + 1) * 512],
                            in0=y_acc[:, k * 1024 + seg * 512: k * 1024 + (seg + 1) * 512],
                            in1=h.psB[:, (k % 2) * 1024 + seg * 512:
                                      (k % 2) * 1024 + (seg + 1) * 512]))
                vv = c.cnt["V"]
                c.wait("t", "V", vv)
            wguardT[e % 2] = c.cnt["T"]
            if e + 1 <= NE:
                wready[(e + 1) % 2] = load_expert(e + 1, wguardT[(e + 1) % 2])
        if g < NG - 1:
            wready[0] = load_expert(0, wguardT[0])

        for k in range(4):
            c.wait("t", "V", c.cnt["V"])
            for seg in range(2):
                c.inc("t", lambda eng, k=k, seg=seg: eng.matmul(
                    out=h.psB[:, seg * 512:(seg + 1) * 512],
                    lhsT=combT[0:8, k * 128:(k + 1) * 128],
                    rhs=b2sb[:, seg * 512:(seg + 1) * 512], start=True, stop=True))
            tv = c.cnt["T"]
            lv = c.inc("y", lambda eng, s=x1_t[g * 4 + k]: eng.dma_start(out=xtile[:], in_=s))
            c.wait("v", "T", tv)
            c.inc("v", lambda eng, k=k: eng.tensor_add(out=y_acc[:, k * 1024:(k + 1) * 1024],
                                                       in0=y_acc[:, k * 1024:(k + 1) * 1024],
                                                       in1=h.psB[:, 0:1024]))
            c.inc("v", lambda eng, k=k: eng.tensor_add(out=y_acc[:, k * 1024:(k + 1) * 1024],
                                                       in0=y_acc[:, k * 1024:(k + 1) * 1024],
                                                       in1=h.sb2_b[:]))
            c.wait("v", "ld", lv)
            vv = c.inc("v", lambda eng, k=k: eng.tensor_add(
                out=y_acc[:, k * 1024:(k + 1) * 1024],
                in0=y_acc[:, k * 1024:(k + 1) * 1024], in1=xtile[:]))
            c.wait("t", "V", vv)
            c.wait("g", "V", vv)
            stv = c.inc("g", lambda eng, d=x1_t[g * 4 + k], s=y_acc[:, k * 1024:(k + 1) * 1024]:
                        eng.dma_start(out=d, in_=s), sname="st")
            c.wait("y", "st", stv)
            c.wait("v", "st", stv)


def emit_ln_full(c, h, es, io, scr, src_name, xn_fm, pitch, do_xprev=False):
    nc = c.nc
    xtile = es.enter_context(nc.sbuf_tensor("l_xtile", [128, 1024], F32))
    off = pitch - 2048
    src_t = io[src_name].rearrange("(n p) d -> n p d", p=128)
    for ti in range(NTILE):
        c.wait("y", "V", c.cnt["V"])
        c.wait("y", "S", c.cnt["S"])
        lv = c.inc("y", lambda eng, s=src_t[ti]: eng.dma_start(out=xtile[:], in_=s))
        c.wait("v", "ld", lv)
        c.wait("s", "ld", lv)
        emit_ln_rows(c, h, xtile[:], 128, scr,
                     lambda j, ti=ti: xn_fm[:, j * pitch + off + ti * 128:
                                            j * pitch + off + (ti + 1) * 128])
    if do_xprev:
        xp = es.enter_context(nc.sbuf_tensor("l_xp", [1, 1024], F32))
        lv = c.inc("y", lambda eng: eng.dma_start(out=xp[:], in_=io["xprev"]))
        c.wait("v", "ld", lv)
        c.wait("s", "ld", lv)
        emit_ln_rows(c, h, xp[:], 1, scr,
                     lambda j: xn_fm[:, j * pitch: j * pitch + 1])


def emit_proj_fm(c, h, es, io, W_sb, src_fm, pitch, dst_fm, scale=None, act=None):
    for kc in range(8):
        for g in range(NG):
            seg = g % 4
            c.wait("t", "V", c.cnt["V"]) if seg == 0 else None
            for dc in range(8):
                c.inc("t", lambda eng,
                      l=W_sb[:, dc * 1024 + kc * 128: dc * 1024 + kc * 128 + 128],
                      r=src_fm[:, dc * pitch + (pitch - 2048) + g * 512:
                               dc * pitch + (pitch - 2048) + (g + 1) * 512],
                      o=h.psB[:, seg * 512:(seg + 1) * 512],
                      st=(dc == 0), sp=(dc == 7):
                      eng.matmul(out=o, lhsT=l, rhs=r, start=st, stop=sp))
            tv = c.cnt["T"]
            dst = dst_fm[:, kc * 2048 + g * 512: kc * 2048 + (g + 1) * 512]
            if act is not None:
                c.wait("s", "T", tv)
                sv = c.inc("s", lambda eng, dst=dst, seg=seg, act=act: eng.activation(
                    out=dst, in_=h.psB[:, seg * 512:(seg + 1) * 512], func=act))
                c.wait("t", "S", sv)
            elif scale is not None:
                c.wait("v", "T", tv)
                vv = c.inc("v", lambda eng, dst=dst, seg=seg, scale=scale: eng.tensor_scalar(
                    out=dst, in0=h.psB[:, seg * 512:(seg + 1) * 512], scalar1=scale,
                    scalar2=None, op0=ALU.mult))
                c.wait("t", "V", vv)
            else:
                c.wait("v", "T", tv)
                vv = c.inc("v", lambda eng, dst=dst, seg=seg: eng.tensor_copy(
                    out=dst, in_=h.psB[:, seg * 512:(seg + 1) * 512]))
                c.wait("t", "V", vv)


def emit_attn(c, h, es, io, Q_fm, attn_fm):
    nc = c.nc
    Kp = [es.enter_context(nc.sbuf_tensor(f"a_Kp{b}", [128, 4096], BF16)) for b in range(2)]
    K_tm = es.enter_context(nc.sbuf_tensor("a_Ktm", [128, 2 * 32 * 65], BF16))
    expS = [es.enter_context(nc.sbuf_tensor(f"a_eS{b}", [128, 1024], BF16)) for b in range(2)]
    recb = es.enter_context(nc.sbuf_tensor("a_recb", [1, 1024], F32))
    recbb = es.enter_context(nc.sbuf_tensor("a_recbb", [64, 1024], F32))
    aodd = es.enter_context(nc.sbuf_tensor("a_aodd", [64, 1024], BF16))

    vv = c.inc("v", lambda eng: eng.memset(K_tm[:], 1.0))
    c.wait("t", "V", vv)
    kp_src = io["K_pair"]
    kld = [0, 0]

    def load_kp(j):
        buf = Kp[j % 2]
        c.inc("y", lambda eng, d=buf[:, 0:2048], s=kp_src[j * 128:(j + 1) * 128, :]:
              eng.dma_start(out=d, in_=s))
        return c.inc("y", lambda eng, d=buf[:, 2048:4096],
                     s=kp_src[1024 + j * 128: 1024 + (j + 1) * 128, :]:
                     eng.dma_start(out=d, in_=s))

    kld[0] = load_kp(0)
    klast = [0, 0]
    for j in range(8):
        buf = Kp[j % 2]
        c.wait("t", "ld", kld[j % 2])
        for rnd in range(4):
            c.wait("t", "V", c.cnt["V"])
            for jj in range(8):
                cc = rnd * 8 + jj
                c.inc("t", lambda eng, i=buf[:, cc * 128:(cc + 1) * 128],
                      o=h.psTb[:, jj * 128:(jj + 1) * 128]:
                      eng.transpose(out=o, in_=i, identity=h.idbf[:]))
            tv = c.cnt["T"]
            c.wait("v", "T", tv)
            ps3 = h.psTb[:, 0:1024].rearrange("p (cc q) -> p cc q", q=128)
            for par in range(2):
                base = par * 2080 + rnd * 8 * 65
                dst = K_tm[:, base: base + 8 * 65].rearrange("p (cc q) -> p cc q", q=65)[:, :, 0:64]
                c.inc("v", lambda eng, dst=dst, src=ps3[:, :, par * 64:(par + 1) * 64]:
                      eng.tensor_copy(out=dst, in_=src))
            vv = c.cnt["V"]
            c.wait("t", "V", vv)
        for par in range(2):
            hq = Q_fm[par * 64:(par + 1) * 64, :]
            for qh in range(2):
                qs = j * 2048 + qh * 1024
                expv = [0, 0]
                for cc in range(32):
                    half = cc % 2
                    c.wait("t", "S", expv[half])
                    for q0 in (0, 512):
                        c.inc("t", lambda eng, l=buf[par * 64:(par + 1) * 64, cc * 128:(cc + 1) * 128],
                              r=hq[:, qs + q0: qs + q0 + 512],
                              o=h.psB[:, half * 1024 + q0: half * 1024 + q0 + 512]:
                              eng.matmul(out=o, lhsT=l, rhs=r, start=True, stop=True))
                    tv = c.cnt["T"]
                    c.wait("s", "T", tv)
                    expv[half] = c.inc("s", lambda eng, half=half: eng.activation(
                        out=expS[half][:], in_=h.psB[:, half * 1024:(half + 1) * 1024],
                        func=AF.Exp, bias=h.kv[:, 2:3]))
                    c.wait("t", "S", expv[half])
                    for q0 in (0, 512):
                        c.inc("t", lambda eng,
                              l=K_tm[:, par * 2080 + cc * 65: par * 2080 + (cc + 1) * 65],
                              r=expS[half][:, q0:q0 + 512],
                              o=h.psA[0:65, q0:q0 + 512], st=(cc == 0), sp=(cc == 31):
                              eng.matmul(out=o, lhsT=l, rhs=r, start=st, stop=sp))
                tv = c.cnt["T"]
                c.wait("v", "T", tv)
                vv = c.inc("v", lambda eng: eng.reciprocal(out=recb[:], in_=h.psA[64:65, 0:1024]))
                c.wait("t", "V", vv)
                for q0 in (0, 512):
                    c.inc("t", lambda eng, q0=q0: eng.matmul(
                        out=h.psB[0:64, q0:q0 + 512], lhsT=h.ones_row[0:1, 0:64],
                        rhs=recb[0:1, q0:q0 + 512], start=True, stop=True))
                tv = c.cnt["T"]
                c.wait("v", "T", tv)
                vv = c.inc("v", lambda eng: eng.tensor_copy(out=recbb[:], in_=h.psB[0:64, 0:1024]))
                c.wait("t", "V", vv)
                if par == 0:
                    dst = attn_fm[0:64, qs:qs + 1024]
                    vv = c.inc("v", lambda eng, dst=dst: eng.tensor_mul(
                        out=dst, in0=h.psA[0:64, 0:1024], in1=recbb[:]))
                    c.wait("t", "V", vv)
                else:
                    vv = c.inc("v", lambda eng: eng.tensor_mul(
                        out=aodd[:], in0=h.psA[0:64, 0:1024], in1=recbb[:]))
                    c.wait("g", "V", vv)
                    stv = c.inc("g", lambda eng, d=attn_fm[64:128, qs:qs + 1024]:
                                eng.dma_start(out=d, in_=aodd[:]), sname="st")
                    c.wait("v", "st", stv)
                    c.wait("t", "V", vv)
        klast[j % 2] = c.cnt["T"]
        if j < 7:
            c.wait("y", "T", klast[(j + 1) % 2])
            kld[(j + 1) % 2] = load_kp(j + 1)


def build_A(dbg=False):
    nc = bass.Bass("TRN2", target_bir_lowering=False, debug=False)
    io = {}
    io["u"] = nc.dram_tensor("u", [T, D], F32, kind="ExternalInput").ap()
    io["uh"] = nc.dram_tensor("uh", [3, D], F32, kind="ExternalInput").ap()
    io["W_inT"] = nc.dram_tensor("W_inT", [D, RIN], F32, kind="ExternalInput").ap()
    for nm, shp in (("conv_w_fm", [128, 72]), ("conv_b_fm", [128, 18]),
                    ("A_log_r", [1, 32]), ("dt_bias_r", [1, 32]),
                    ("ident32", [128, 128]), ("U128", [128, 128]), ("TRIU32", [128, 128]),
                    ("SLT32", [128, 128]),
                    ("ones_row", [1, 128]), ("ones_col", [128, 1])):
        io[nm] = nc.dram_tensor(nm, shp, F32, kind="ExternalInput").ap()
    for nm, shp in (("identbf", [128, 128]), ("SLT", [128, 128])):
        io[nm] = nc.dram_tensor(nm, shp, BF16, kind="ExternalInput").ap()
    io["S_out"] = nc.dram_tensor("S_out", [128, DI], F32, kind="ExternalOutput").ap()
    if dbg:
        for nm, shp, dt in (("dbg_tails", [128, 54], F32),
                            ("dbg_xg", [128, 8 * 256], BF16), ("dbg_xfm", [128, 18 * 256], BF16),
                            ("dbg_dtT", [128, 32], F32), ("dbg_aT", [128, 32], F32),
                            ("dbg_AcumT", [128, 32], F32), ("dbg_wT", [128, 32], F32),
                            ("dbg_eTb", [128, 32], F32), ("dbg_xdw", [128, DI], BF16),
                            ("dbg_Btm", [128, 128], BF16), ("dbg_S", [128, DI], F32)):
            io[nm] = nc.dram_tensor(nm, shp, dt, kind="ExternalOutput").ap()

    es = contextlib.ExitStack()
    c = Prog(nc, es)
    with nc.Block() as blk:
        h = emit_prelude(c, io, es, need=("mamba",))
        scr = _mk_scr(c, es)
        emit_mamba(c, h, es, io, scr, full=False, dbg=dbg)
        c.barrier()
        c.build(blk)
    es.close()
    return nc


def build_B(dbg=False, dbg_tile=(0, 0)):
    nc = bass.Bass("TRN2", target_bir_lowering=False, debug=False)
    io = {}
    io["u"] = nc.dram_tensor("u", [T, D], F32, kind="ExternalInput").ap()
    io["uh"] = nc.dram_tensor("uh", [3, D], F32, kind="ExternalInput").ap()
    io["S_in"] = nc.dram_tensor("S_in", [128, DI], F32, kind="ExternalInput").ap()
    f32_ins = [("W_inT", [D, RIN]), ("W_outT", [DI, D]),
               ("conv_w_fm", [128, 72]), ("conv_b_fm", [128, 18]),
               ("A_log_r", [1, 32]), ("dt_bias_r", [1, 32]), ("D_r", [1, 32]),
               ("norm_w_r", [1, DI]),
               ("gate_wT", [D, 8]), ("moe_w1", [NE, D, INTER]), ("moe_w2", [NE, INTER, D]),
               ("moe_w3", [NE, D, INTER]), ("moe_b1_fm", [NE, 128, 8]),
               ("moe_b3_fm", [NE, 128, 8]), ("moe_b2", [NE, INTER]),
               ("sw1", [D, INTER]), ("sw2", [INTER, D]), ("sw3", [D, INTER]),
               ("sb1_fm", [128, 8]), ("sb3_fm", [128, 8]), ("sb2_r", [1, INTER]),
               ("Wk1", [D, D]),
               ("ident32", [128, 128]), ("U128", [128, 128]), ("TRIU32", [128, 128]),
               ("SLT32", [128, 128]),
               ("ones_row", [1, 128]), ("ones_col", [128, 1])]
    for nm, shp in f32_ins:
        io[nm] = nc.dram_tensor(nm, shp, F32, kind="ExternalInput").ap()
    for nm, shp in (("identbf", [128, 128]), ("SLT", [128, 128])):
        io[nm] = nc.dram_tensor(nm, shp, BF16, kind="ExternalInput").ap()
    io["x1"] = nc.dram_tensor("x1", [T, D], F32, kind="ExternalOutput").ap()
    io["K_own"] = nc.dram_tensor("K_own", [D, T], BF16, kind="ExternalOutput").ap()
    if dbg:
        for nm, shp, dt in (("dbg_tails", [128, 54], F32),
                            ("dbg_xg", [128, 8 * 256], BF16), ("dbg_xfm", [128, 18 * 256], BF16),
                            ("dbg_dtT", [128, 32], F32), ("dbg_aT", [128, 32], F32),
                            ("dbg_AcumT", [128, 32], F32), ("dbg_wT", [128, 32], F32),
                            ("dbg_eTb", [128, 32], F32), ("dbg_xdw", [128, DI], BF16),
                            ("dbg_Btm", [128, 128], BF16), ("dbg_S", [128, DI], F32),
                            ("dbg_xd", [128, DI], BF16), ("dbg_uT", [128, 32], F32),
                            ("dbg_siluz", [128, DI], BF16), ("dbg_t1", [128, DI], F32),
                            ("dbg_yn", [128, DI], BF16), ("dbg_CBm", [128, 128], BF16),
                            ("dbg_pre", [128, DI], F32), ("dbg_gate", [128, DI], F32),
                            ("dbg_Db", [128, 32], F32), ("dbg_normwb", [128, DI], F32),
                            ("dbg_rms", [128, 4], F32), ("dbg_Mtq", [128, 1024], BF16),
                            ("dbg_amq", [128, 1024], F32)):
            io[nm] = nc.dram_tensor(nm, shp, dt, kind="ExternalOutput").ap()

    es = contextlib.ExitStack()
    c = Prog(nc, es)
    with nc.Block() as blk:
        h = emit_prelude(c, io, es, need=("mamba", "mamba_full", "moe"))
        scr = _mk_scr(c, es)
        with contextlib.ExitStack() as es_m:
            emit_mamba(c, h, es_m, io, scr, full=True, dbg=dbg, dbg_tile=dbg_tile)
            c.barrier()
        with contextlib.ExitStack() as es_e:
            emit_moe(c, h, es_e, io, scr)
            c.barrier()
        with contextlib.ExitStack() as es_k:
            xn1 = es_k.enter_context(nc.sbuf_tensor("b_xn1", [128, 8 * 2049], BF16))
            emit_ln_full(c, h, es_k, io, scr, "x1", xn1, 2049, do_xprev=False)
            c.barrier()
            Wk = emit_load_wchunks(c, es_k, io, "Wk1", D, D, tag="wk1")
            Ko = es_k.enter_context(nc.sbuf_tensor("b_Ko", [128, 8 * 2048], BF16))
            emit_proj_fm(c, h, es_k, io, Wk, xn1[:], 2049, Ko[:])
            c.barrier()
            for kc in range(8):
                c.inc("g", lambda eng, d=io["K_own"][kc * 128:(kc + 1) * 128, :],
                      s=Ko[:, kc * 2048:(kc + 1) * 2048]: eng.dma_start(out=d, in_=s),
                      sname="st")
            c.wait("g", "st", c.cnt["st"])
        c.barrier()
        c.build(blk)
    es.close()
    return nc


def build_C():
    nc = bass.Bass("TRN2", target_bir_lowering=False, debug=False)
    io = {}
    io["x1"] = nc.dram_tensor("x1", [T, D], F32, kind="ExternalInput").ap()
    io["xprev"] = nc.dram_tensor("xprev", [1, D], F32, kind="ExternalInput").ap()
    io["K_pair"] = nc.dram_tensor("K_pair", [2048, T], BF16, kind="ExternalInput").ap()
    for nm, shp in (("Wq1", [D, D]), ("Wo1", [D, D]), ("bo_r", [1, D]),
                    ("WrT", [D, D]), ("WkT", [D, D]), ("WvT", [D, D]),
                    ("mu_r_fm", [128, 8]), ("mu_k_fm", [128, 8]),
                    ("ident32", [128, 128]), ("ones_row", [1, 128]), ("ones_col", [128, 1])):
        io[nm] = nc.dram_tensor(nm, shp, F32, kind="ExternalInput").ap()
    io["identbf"] = nc.dram_tensor("identbf", [128, 128], BF16, kind="ExternalInput").ap()
    io["out"] = nc.dram_tensor("out", [T, D], F32, kind="ExternalOutput").ap()

    es = contextlib.ExitStack()
    c = Prog(nc, es)
    with nc.Block() as blk:
        h = emit_prelude(c, io, es, need=("l1",))
        scr = _mk_scr(c, es)
        xn1 = es.enter_context(nc.sbuf_tensor("c_xn1", [128, 8 * 2049], BF16))
        emit_ln_full(c, h, es, io, scr, "x1", xn1, 2049, do_xprev=True)
        c.barrier()
        x1_t = io["x1"].rearrange("(n p) d -> n p d", p=128)
        out_t = io["out"].rearrange("(n p) d -> n p d", p=128)
        with contextlib.ExitStack() as es_a:
            Wq = emit_load_wchunks(c, es_a, io, "Wq1", D, D, tag="wq1")
            Q_fm = es_a.enter_context(nc.sbuf_tensor("c_Q", [128, 8 * 2048], BF16))
            emit_proj_fm(c, h, es_a, io, Wq, xn1[:], 2049, Q_fm[:], scale=1.0 / 32.0)
            c.barrier()
            attn_fm = es_a.enter_context(nc.sbuf_tensor("c_attn", [128, 8 * 2048], BF16))
            emit_attn(c, h, es_a, io, Q_fm[:], attn_fm[:])
            c.barrier()
            Wo = emit_load_wchunks(c, es_a, io, "Wo1", D, D, tag="wo1")
            xtile = es_a.enter_context(nc.sbuf_tensor("c_xt", [128, 1024], F32))
            xo = es_a.enter_context(nc.sbuf_tensor("c_xo", [128, 1024], F32))
            for ti in range(NTILE):
                c.wait("y", "st", c.cnt["st"])
                lv = c.inc("y", lambda eng, s=x1_t[ti]: eng.dma_start(out=xtile[:], in_=s))
                c.wait("t", "V", c.cnt["V"])
                for seg in range(2):
                    for ec in range(8):
                        c.inc("t", lambda eng, ec=ec, seg=seg, ti=ti, st=(ec == 0), sp=(ec == 7):
                              eng.matmul(
                                  out=h.psB[:, seg * 512:(seg + 1) * 512],
                                  lhsT=attn_fm[:, ec * 2048 + ti * 128: ec * 2048 + (ti + 1) * 128],
                                  rhs=Wo[:, ec * 1024 + seg * 512: ec * 1024 + (seg + 1) * 512],
                                  start=st, stop=sp))
                tv = c.cnt["T"]
                c.wait("v", "T", tv)
                c.wait("v", "ld", lv)
                c.inc("v", lambda eng: eng.tensor_add(out=xo[:], in0=h.psB[:, 0:1024],
                                                      in1=h.bo_b[:]))
                vv = c.inc("v", lambda eng: eng.tensor_add(out=xo[:], in0=xo[:], in1=xtile[:]))
                c.wait("t", "V", vv)
                c.wait("g", "V", vv)
                stv = c.inc("g", lambda eng, d=out_t[ti]: eng.dma_start(out=d, in_=xo[:]),
                            sname="st")
                c.wait("v", "st", stv)
        c.barrier()
        with contextlib.ExitStack() as es_f:
            r_all = es_f.enter_context(nc.sbuf_tensor("c_rall", [128, 16 * 1024], BF16))
            with contextlib.ExitStack() as es_r:
                WrT = emit_load_wchunks(c, es_r, io, "WrT", D, D, tag="wrt")
                xr = es_r.enter_context(nc.sbuf_tensor("c_xr", [128, 8 * 2048], BF16))
                for dc in range(8):
                    c.inc("v", lambda eng, dc=dc: eng.tensor_scalar(
                        out=xr[:, dc * 2048:(dc + 1) * 2048],
                        in0=xn1[:, dc * 2049: dc * 2049 + 2048],
                        scalar1=h.omur[:, dc:dc + 1], scalar2=None, op0=ALU.mult))
                    vv = c.inc("v", lambda eng, dc=dc: eng.scalar_tensor_tensor(
                        out=xr[:, dc * 2048:(dc + 1) * 2048],
                        in0=xn1[:, dc * 2049 + 1: dc * 2049 + 2049],
                        scalar=h.mur[:, dc:dc + 1],
                        in1=xr[:, dc * 2048:(dc + 1) * 2048], op0=ALU.mult, op1=ALU.add))
                c.wait("t", "V", vv)
                for ti in range(NTILE):
                    for seg in range(2):
                        for dc in range(8):
                            c.inc("t", lambda eng, dc=dc, seg=seg, ti=ti, st=(dc == 0), sp=(dc == 7):
                                  eng.matmul(
                                      out=h.psB[:, seg * 512:(seg + 1) * 512],
                                      lhsT=xr[:, dc * 2048 + ti * 128: dc * 2048 + (ti + 1) * 128],
                                      rhs=WrT[:, dc * 1024 + seg * 512: dc * 1024 + (seg + 1) * 512],
                                      start=st, stop=sp))
                    tv = c.cnt["T"]
                    c.wait("s", "T", tv)
                    sv = c.inc("s", lambda eng, ti=ti: eng.activation(
                        out=r_all[:, ti * 1024:(ti + 1) * 1024], in_=h.psB[:, 0:1024],
                        func=AF.Sigmoid))
                    c.wait("t", "S", sv)
            c.barrier()
            kact = es_f.enter_context(nc.sbuf_tensor("c_kact", [128, 8 * 2048], BF16))
            with contextlib.ExitStack() as es_k2:
                WkT = emit_load_wchunks(c, es_k2, io, "WkT", D, D, tag="wkt")
                xk = es_k2.enter_context(nc.sbuf_tensor("c_xk", [128, 8 * 2048], BF16))
                scrk = es_k2.enter_context(nc.sbuf_tensor("c_scrk", [128, 512], F32))
                for dc in range(8):
                    c.inc("v", lambda eng, dc=dc: eng.tensor_scalar(
                        out=xk[:, dc * 2048:(dc + 1) * 2048],
                        in0=xn1[:, dc * 2049: dc * 2049 + 2048],
                        scalar1=h.omuk[:, dc:dc + 1], scalar2=None, op0=ALU.mult))
                    vv = c.inc("v", lambda eng, dc=dc: eng.scalar_tensor_tensor(
                        out=xk[:, dc * 2048:(dc + 1) * 2048],
                        in0=xn1[:, dc * 2049 + 1: dc * 2049 + 2049],
                        scalar=h.muk[:, dc:dc + 1],
                        in1=xk[:, dc * 2048:(dc + 1) * 2048], op0=ALU.mult, op1=ALU.add))
                c.wait("t", "V", vv)
                for kc in range(8):
                    for g in range(NG):
                        seg = g % 4
                        for dc in range(8):
                            c.inc("t", lambda eng, dc=dc, kc=kc, g=g, seg=seg,
                                  st=(dc == 0), sp=(dc == 7):
                                  eng.matmul(
                                      out=h.psB[:, seg * 512:(seg + 1) * 512],
                                      lhsT=WkT[:, dc * 1024 + kc * 128: dc * 1024 + kc * 128 + 128],
                                      rhs=xk[:, dc * 2048 + g * 512: dc * 2048 + (g + 1) * 512],
                                      start=st, stop=sp))
                        tv = c.cnt["T"]
                        c.wait("s", "T", tv)
                        c.inc("s", lambda eng, seg=seg: eng.activation(
                            out=scrk[:], in_=h.psB[:, seg * 512:(seg + 1) * 512], func=AF.Relu))
                        sv = c.inc("s", lambda eng, kc=kc, g=g: eng.activation(
                            out=kact[:, kc * 2048 + g * 512: kc * 2048 + (g + 1) * 512],
                            in_=scrk[:], func=AF.Square))
                        c.wait("t", "S", sv)
            c.barrier()
            with contextlib.ExitStack() as es_v:
                WvT = emit_load_wchunks(c, es_v, io, "WvT", D, D, tag="wvt")
                otile = es_v.enter_context(nc.sbuf_tensor("c_ot", [128, 1024], F32))
                fout = es_v.enter_context(nc.sbuf_tensor("c_fout", [128, 1024], F32))
                for ti in range(NTILE):
                    c.wait("y", "st", c.cnt["st"])
                    lv = c.inc("y", lambda eng, s=out_t[ti]: eng.dma_start(out=otile[:], in_=s))
                    c.wait("t", "V", c.cnt["V"])
                    for seg in range(2):
                        for ic in range(8):
                            c.inc("t", lambda eng, ic=ic, seg=seg, ti=ti, st=(ic == 0), sp=(ic == 7):
                                  eng.matmul(
                                      out=h.psB[:, seg * 512:(seg + 1) * 512],
                                      lhsT=kact[:, ic * 2048 + ti * 128: ic * 2048 + (ti + 1) * 128],
                                      rhs=WvT[:, ic * 1024 + seg * 512: ic * 1024 + (seg + 1) * 512],
                                      start=st, stop=sp))
                    tv = c.cnt["T"]
                    c.wait("v", "T", tv)
                    c.wait("v", "ld", lv)
                    c.inc("v", lambda eng, ti=ti: eng.tensor_mul(
                        out=fout[:], in0=h.psB[:, 0:1024],
                        in1=r_all[:, ti * 1024:(ti + 1) * 1024]))
                    vv = c.inc("v", lambda eng: eng.tensor_add(out=fout[:], in0=fout[:],
                                                               in1=otile[:]))
                    c.wait("t", "V", vv)
                    c.wait("g", "V", vv)
                    stv = c.inc("g", lambda eng, d=out_t[ti]: eng.dma_start(out=d, in_=fout[:]),
                                sname="st")
                    c.wait("v", "st", stv)
        c.barrier()
        c.wait("g", "st", c.cnt["st"])
        c.build(blk)
    es.close()
    return nc


_CACHE = {}


def _get(name, fn):
    if name not in _CACHE:
        _CACHE[name] = fn()
    return _CACHE[name]


def kernel(**inputs):
    w = prep_weights(inputs)
    u = np.ascontiguousarray(np.asarray(inputs["L0_attn_u"], dtype=np.float32))
    cores = list(range(8))
    u_c, uh_c = [], []
    for ci in cores:
        b, par = ci // 2, ci % 2
        u_c.append(np.ascontiguousarray(u[b, par * T:(par + 1) * T]))
        if par == 1:
            uh_c.append(np.ascontiguousarray(u[b, T - 3:T]))
        else:
            uh_c.append(np.zeros((3, D), np.float32))

    wA = {k: w[k] for k in ("W_inT", "conv_w_fm", "conv_b_fm", "A_log_r", "dt_bias_r",
                            "ident32", "U128", "TRIU32", "SLT32", "ones_row", "ones_col",
                            "identbf", "SLT")}
    ncA = _get("A", build_A)
    mapsA = [dict(u=u_c[ci], uh=uh_c[ci], **wA) for ci in cores]
    resA = run_bass_kernel_spmd(ncA, mapsA, core_ids=cores)
    S_in = []
    for ci in cores:
        if ci % 2 == 0:
            S_in.append(np.zeros((128, DI), np.float32))
        else:
            S_in.append(np.ascontiguousarray(resA.results[ci - 1]["S_out"]))

    wB = {k: w[k] for k in ("W_inT", "W_outT", "conv_w_fm", "conv_b_fm", "A_log_r",
                            "dt_bias_r", "D_r", "norm_w_r", "gate_wT", "moe_w1", "moe_w2",
                            "moe_w3", "moe_b1_fm", "moe_b3_fm", "moe_b2", "sw1", "sw2",
                            "sw3", "sb1_fm", "sb3_fm", "sb2_r", "Wk1",
                            "ident32", "U128", "TRIU32", "SLT32", "ones_row", "ones_col",
                            "identbf", "SLT")}
    ncB = _get("B", build_B)
    mapsB = [dict(u=u_c[ci], uh=uh_c[ci], S_in=S_in[ci], **wB) for ci in cores]
    resB = run_bass_kernel_spmd(ncB, mapsB, core_ids=cores)
    x1 = [np.ascontiguousarray(resB.results[ci]["x1"]) for ci in cores]
    K_own = [np.asarray(resB.results[ci]["K_own"]) for ci in cores]

    wC = {k: w[k] for k in ("Wq1", "Wo1", "bo_r", "WrT", "WkT", "WvT", "mu_r_fm",
                            "mu_k_fm", "ident32", "ones_row", "ones_col", "identbf")}
    ncC = _get("C", build_C)
    mapsC = []
    for ci in cores:
        b, par = ci // 2, ci % 2
        K_pair = np.ascontiguousarray(
            np.concatenate([K_own[2 * b], K_own[2 * b + 1]], axis=0))
        xprev = (x1[2 * b][T - 1:T] if par == 1 else np.zeros((1, D), np.float32))
        mapsC.append(dict(x1=x1[ci], xprev=np.ascontiguousarray(xprev), K_pair=K_pair, **wC))
    resC = run_bass_kernel_spmd(ncC, mapsC, core_ids=cores)

    out = np.empty((4, 4096, D), np.float32)
    for ci in cores:
        b, par = ci // 2, ci % 2
        out[b, par * T:(par + 1) * T] = resC.results[ci]["out"]
    return out


# version 03 (final)
# speedup vs baseline: 1.0036x; 1.0036x over previous
import contextlib
import numpy as np
import ml_dtypes
import concourse.bass as bass
import concourse.mybir as mybir
from concourse.bass_utils import run_bass_kernel_spmd

F32 = mybir.dt.float32
BF16 = mybir.dt.bfloat16
AF = mybir.ActivationFunctionType
ALU = mybir.AluOpType
AX = mybir.AxisListType

D = 1024
DI = 2048
DS = 128
NH = 32
HD = 64
CONVD = 2304
RIN = 4384
NE = 8
INTER = 1024
T = 2048
NTILE = 16
NG = 4
GS = 512
EPS = 1e-5

_ENGS = ("t", "v", "s", "g", "y")
_ENG_ATTR = {"t": "tensor", "v": "vector", "s": "scalar", "g": "gpsimd", "y": "sync"}



_NAME_N = [0]


def _sbt(nc, name, shape, dtype, **kw):
    _NAME_N[0] += 1
    return nc.sbuf_tensor(f"{name}_{_NAME_N[0]}", shape, dtype, **kw)


class Prog:
    def __init__(self, nc, es):
        self.nc = nc
        self.q = {e: [] for e in _ENGS}
        self.sem = {}
        self.cnt = {}
        for name in ("ld", "st", "T", "V", "S", "G"):
            self.sem[name] = es.enter_context(nc.semaphore(name))
            self.cnt[name] = 0

    def emit(self, e, f):
        self.q[e].append(f)

    def inc(self, e, f, sname=None, n=None):
        if sname is None:
            sname = {"t": "T", "v": "V", "s": "S", "g": "G", "y": "ld"}[e]
        if n is None:
            n = 16 if sname in ("ld", "st") else 1
        self.cnt[sname] += n
        sem = self.sem[sname]
        if e in ("v", "s") or (e == "g" and sname == "G"):
            self.q[e].append(lambda eng, f=f: f(eng))
            self.q[e].append(lambda eng, sem=sem, n=n: eng.drain().then_inc(sem, n))
        else:
            self.q[e].append(lambda eng, f=f, sem=sem, n=n: f(eng).then_inc(sem, n))
        return self.cnt[sname]

    def inc_direct(self, e, f, sname=None, n=None):
        if sname is None:
            sname = {"t": "T", "v": "V", "s": "S", "g": "G", "y": "ld"}[e]
        if n is None:
            n = 16 if sname in ("ld", "st") else 1
        self.cnt[sname] += n
        sem = self.sem[sname]
        self.q[e].append(lambda eng, f=f, sem=sem, n=n: f(eng).then_inc(sem, n))
        return self.cnt[sname]

    def incd(self, e, f, sname=None, n=None):
        if sname is None:
            sname = {"t": "T", "v": "V", "s": "S", "g": "G", "y": "ld"}[e]
        if n is None:
            n = 16 if sname in ("ld", "st") else 1
        self.q[e].append(lambda eng, f=f: f(eng))
        self.cnt[sname] += n
        sem = self.sem[sname]
        self.q[e].append(lambda eng, sem=sem, n=n: eng.drain().then_inc(sem, n))
        return self.cnt[sname]

    def wait(self, e, sname, val):
        if val <= 0:
            return
        sem = self.sem[sname]
        self.q[e].append(lambda eng, sem=sem, val=val: eng.wait_ge(sem, val))

    def barrier(self):
        marks = {}
        for e, sname in (("t", "T"), ("v", "V"), ("s", "S"), ("g", "G")):
            marks[sname] = self.inc(e, lambda eng: eng.nop(), sname=sname, n=1)
        marks["ld"] = self.cnt["ld"]
        marks["st"] = self.cnt["st"]
        for e in _ENGS:
            for sname, val in marks.items():
                self.wait(e, sname, val)

    def build(self, block):
        for e in _ENGS:
            fs = self.q[e]
            getattr(block, _ENG_ATTR[e])(lambda eng, fs=fs: [f(eng) for f in fs])


def bcast_ap(sb_ap, reps, rep_inner):
    return sb_ap.unsqueeze(2).broadcast_to([sb_ap.shape[0], reps, rep_inner])


def _fm(v, nchunk):
    return np.ascontiguousarray(v.reshape(nchunk, 128).T)


def prep_weights(i):
    w = {}
    w["W_inT"] = np.ascontiguousarray(i["L0_attn_W_in"].T)
    w["conv_w_fm"] = np.ascontiguousarray(
        np.asarray(i["L0_attn_conv_w"]).reshape(18, 128, 4).transpose(1, 0, 2).reshape(128, 72))
    w["conv_b_fm"] = _fm(np.asarray(i["L0_attn_conv_b"]), 18)
    w["A_log_r"] = np.asarray(i["L0_attn_A_log"]).reshape(1, NH)
    w["dt_bias_r"] = np.asarray(i["L0_attn_dt_bias"]).reshape(1, NH)
    w["D_r"] = np.asarray(i["L0_attn_D"]).reshape(1, NH)
    w["norm_w_r"] = np.asarray(i["L0_attn_norm_w"]).reshape(1, DI)
    w["W_outT"] = np.ascontiguousarray(i["L0_attn_W_out"].T)
    w["gate_wT"] = np.ascontiguousarray(i["L0_ffn_gate_w"].T)
    w["moe_w1"] = np.asarray(i["L0_ffn_w1"])
    w["moe_w2"] = np.asarray(i["L0_ffn_w2"])
    w["moe_w3"] = np.asarray(i["L0_ffn_w3"])
    w["moe_b1_fm"] = np.stack([_fm(np.asarray(i["L0_ffn_b1"])[e], 8) for e in range(NE)])
    w["moe_b3_fm"] = np.stack([_fm(np.asarray(i["L0_ffn_b3"])[e], 8) for e in range(NE)])
    w["moe_b2"] = np.asarray(i["L0_ffn_b2"])
    w["sw1"] = np.asarray(i["L0_ffn_sw1"])
    w["sw2"] = np.asarray(i["L0_ffn_sw2"])
    w["sw3"] = np.asarray(i["L0_ffn_sw3"])
    w["sb1_fm"] = _fm(np.asarray(i["L0_ffn_sb1"]), 8)
    w["sb3_fm"] = _fm(np.asarray(i["L0_ffn_sb3"]), 8)
    w["sb2_r"] = np.asarray(i["L0_ffn_sb2"]).reshape(1, INTER)
    w["Wk1"] = np.asarray(i["L1_attn_Wk"])
    w["Wq1"] = np.asarray(i["L1_attn_Wq"])
    w["Wo1"] = np.asarray(i["L1_attn_Wo"])
    w["bo_r"] = np.asarray(i["L1_attn_bo"]).reshape(1, D)
    w["WrT"] = np.ascontiguousarray(i["L1_ffn_W_r"].T)
    w["WkT"] = np.ascontiguousarray(i["L1_ffn_W_k"].T)
    w["WvT"] = np.ascontiguousarray(i["L1_ffn_W_v"].T)
    w["mu_r_fm"] = _fm(np.asarray(i["L1_ffn_mu_r"]), 8)
    w["mu_k_fm"] = _fm(np.asarray(i["L1_ffn_mu_k"]), 8)
    w["ident32"] = np.eye(128, dtype=np.float32)
    w["identbf"] = np.eye(128).astype(ml_dtypes.bfloat16)
    w["U128"] = np.triu(np.ones((128, 128), np.float32))
    w["SLT"] = np.tril(np.ones((128, 128)), -1).astype(ml_dtypes.bfloat16)
    w["SLT32"] = np.tril(np.ones((128, 128), np.float32), -1)
    w["TRIU32"] = np.triu(np.ones((128, 128), np.float32))
    w["ones_row"] = np.ones((1, 128), np.float32)
    w["ones_col"] = np.ones((128, 1), np.float32)
    return {k: (v if v.dtype in (np.dtype(np.float32), np.dtype(ml_dtypes.bfloat16))
                else v.astype(np.float32)) for k, v in w.items()}


class Common:
    pass


def emit_prelude(c, io, es, need=()):
    nc = c.nc
    h = Common()
    es_rows = contextlib.ExitStack()
    h.psB = es.enter_context(nc.psum_tensor("psB", [128, 2048], F32))
    h.psA = es.enter_context(nc.psum_tensor("psA", [128, 1024], F32))
    h.psTb = es.enter_context(nc.psum_tensor("psTb", [128, 1024], BF16))
    h.id32 = es.enter_context(_sbt(nc, "c_id32", [128, 128], F32))
    h.idbf = es.enter_context(_sbt(nc, "c_idbf", [128, 128], BF16))
    h.ones_row = es.enter_context(_sbt(nc, "c_ones_row", [1, 128], F32))
    h.ones_col = es.enter_context(_sbt(nc, "c_ones_col", [128, 1], F32))
    h.kv = es.enter_context(_sbt(nc, "c_kv", [128, 4], F32))
    vv0 = c.inc("v", lambda eng: eng.memset(h.kv[:, 0:1], EPS))
    vv0 = c.inc("v", lambda eng: eng.memset(h.kv[:, 1:2], 1.0))
    vv0 = c.inc("v", lambda eng: eng.memset(h.kv[:, 2:3], -20.0))
    c.wait("s", "V", vv0)
    loads = [(h.id32, "ident32"), (h.idbf, "identbf"),
             (h.ones_row, "ones_row"), (h.ones_col, "ones_col")]
    if "mamba" in need:
        h.U128 = es.enter_context(_sbt(nc, "c_U128", [128, 128], F32))
        h.SLT = es.enter_context(_sbt(nc, "c_SLT", [128, 128], BF16))
        h.TRIU32 = es.enter_context(_sbt(nc, "c_TRIU32", [128, 128], F32))
        h.SLT32 = es.enter_context(_sbt(nc, "c_SLT32", [128, 128], F32))
        h.convw = es.enter_context(_sbt(nc, "c_convw", [128, 72], F32))
        h.convb = es.enter_context(_sbt(nc, "c_convb", [128, 18], F32))
        h.row32 = es_rows.enter_context(_sbt(nc, "c_row32", [1, 32], F32, side="right"))
        h.row32b = es_rows.enter_context(_sbt(nc, "c_row32b", [1, 32], F32, side="right"))
        h.Aneg_b = es.enter_context(_sbt(nc, "c_Aneg_b", [128, 32], F32))
        h.dtb_b = es.enter_context(_sbt(nc, "c_dtb_b", [128, 32], F32))
        loads += [(h.U128, "U128"), (h.SLT, "SLT"), (h.TRIU32, "TRIU32"),
                  (h.SLT32, "SLT32"),
                  (h.convw, "conv_w_fm"), (h.convb, "conv_b_fm")]
    if "mamba_full" in need:
        h.D_b = es.enter_context(_sbt(nc, "c_D_b", [128, 32], F32))
        h.normw_b = es.enter_context(_sbt(nc, "c_normw_b", [128, DI], F32))
        h.rowDI = es_rows.enter_context(_sbt(nc, "c_rowDI", [1, DI], F32, side="right"))
    if "moe" in need:
        h.sb2_b = es.enter_context(_sbt(nc, "c_sb2_b", [128, INTER], F32))
        h.rowI = es_rows.enter_context(_sbt(nc, "c_rowI", [1, INTER], F32, side="right"))
    if "l1" in need:
        h.bo_b = es.enter_context(_sbt(nc, "c_bo_b", [128, D], F32))
        h.rowD = es_rows.enter_context(_sbt(nc, "c_rowD", [1, D], F32, side="right"))
        h.mur = es.enter_context(_sbt(nc, "c_mur", [128, 8], F32))
        h.muk = es.enter_context(_sbt(nc, "c_muk", [128, 8], F32))
        h.omur = es.enter_context(_sbt(nc, "c_omur", [128, 8], F32))
        h.omuk = es.enter_context(_sbt(nc, "c_omuk", [128, 8], F32))
        loads += [(h.mur, "mu_r_fm"), (h.muk, "mu_k_fm")]

    for sb, name in loads:
        c.inc("y", lambda eng, sb=sb, ap=io[name]: eng.dma_start(out=sb[:], in_=ap))
    ldv = c.cnt["ld"]
    for e in ("t", "v", "s"):
        c.wait(e, "ld", ldv)

    def bc_row(src_row_ap, ncols, dst_sb_ap):
        tv = None
        for s0 in range(0, ncols, 512):
            s1 = min(s0 + 512, ncols)
            tv = c.inc("t", lambda eng, a=src_row_ap[0:1, s0:s1], o=h.psB[:, s0:s1]:
                       eng.matmul(out=o, lhsT=h.ones_row[:], rhs=a, start=True, stop=True))
        c.wait("v", "T", tv)
        vv = c.inc("v", lambda eng, o=dst_sb_ap[:, 0:ncols], i=h.psB[:, 0:ncols]:
                   eng.tensor_copy(out=o, in_=i))
        c.wait("t", "V", vv)

    if "mamba" in need:
        c.inc("y", lambda eng: eng.dma_start(out=h.row32[:], in_=io["A_log_r"]))
        c.inc("y", lambda eng: eng.dma_start(out=h.row32b[:], in_=io["dt_bias_r"]))
        for e in ("t", "v", "s"):
            c.wait(e, "ld", c.cnt["ld"])
        bc_row(h.row32[:], 32, h.Aneg_b[:])
        bc_row(h.row32b[:], 32, h.dtb_b[:])
        sv = c.inc("s", lambda eng: eng.activation(out=h.Aneg_b[:], in_=h.Aneg_b[:], func=AF.Exp))
        c.wait("v", "S", sv)
        vv = c.inc("v", lambda eng: eng.tensor_scalar(out=h.Aneg_b[:], in0=h.Aneg_b[:],
                                                      scalar1=-1.0, scalar2=None, op0=ALU.mult))
        c.wait("t", "V", vv)
    if "mamba_full" in need:
        c.wait("y", "T", c.cnt["T"])
        c.inc("y", lambda eng: eng.dma_start(out=h.row32b[:], in_=io["D_r"]))
        c.inc("y", lambda eng: eng.dma_start(out=h.rowDI[:], in_=io["norm_w_r"]))
        for e in ("t", "v"):
            c.wait(e, "ld", c.cnt["ld"])
        bc_row(h.row32b[:], 32, h.D_b[:])
        bc_row(h.rowDI[:], DI, h.normw_b[:])
    if "moe" in need:
        c.inc("y", lambda eng: eng.dma_start(out=h.rowI[:], in_=io["sb2_r"]))
        for e in ("t", "v"):
            c.wait(e, "ld", c.cnt["ld"])
        bc_row(h.rowI[:], INTER, h.sb2_b[:])
    if "l1" in need:
        c.inc("y", lambda eng: eng.dma_start(out=h.rowD[:], in_=io["bo_r"]))
        for e in ("t", "v"):
            c.wait(e, "ld", c.cnt["ld"])
        bc_row(h.rowD[:], D, h.bo_b[:])
        for src, dst in ((h.mur, h.omur), (h.muk, h.omuk)):
            c.inc("v", lambda eng, s=src, d=dst: eng.tensor_scalar(
                out=d[:], in0=s[:], scalar1=-1.0, scalar2=1.0, op0=ALU.mult, op1=ALU.add))
    c.barrier()
    es_rows.close()
    return h


def emit_load_wchunks(c, es, io, name, rows, cols, tag=None, colseg=1096):
    nc = c.nc
    nch = rows // 128
    tag = tag or name
    dst = es.enter_context(_sbt(nc, f"w_{tag}", [128, nch * cols], BF16))
    segs = [(s, min(s + colseg, cols)) for s in range(0, cols, colseg)]
    with _sbt(nc, f"stg_{tag}", [128, segs[0][1]], F32, side="right") as stage:
        src = io[name].rearrange("(c p) d -> c p d", p=128)
        for ch in range(nch):
            for (s0, s1) in segs:
                c.wait("y", "G", c.cnt["G"])
                lv = c.inc("y", lambda eng, s=src[ch][:, s0:s1], d=stage[:, 0:s1 - s0]:
                           eng.dma_start(out=d, in_=s))
                c.wait("g", "ld", lv)
                c.inc("g", lambda eng, d=dst[:, ch * cols + s0: ch * cols + s1],
                      s=stage[:, 0:s1 - s0]: eng.tensor_copy(out=d, in_=s))
    gv = c.cnt["G"]
    for e in ("t", "v", "s"):
        c.wait(e, "G", gv)
    return dst


def emit_ln_rows(c, h, x_ap, nrows, scr, xn_dst_fn, dst_cast=True):
    m, sq, rstd, xn_tm = scr["m"], scr["sq"], scr["rstd"], scr["xn_tm"]
    R = nrows
    c.wait("s", "T", c.cnt["T"])
    c.inc("v", lambda eng: eng.reduce_sum(out=m[0:R, 0:1], in_=x_ap, axis=AX.X))
    sv = c.inc("s", lambda eng: eng.activation(out=sq[0:R, :], in_=x_ap, func=AF.Square,
                                               accum_out=m[0:R, 1:2]))
    c.wait("v", "S", sv)
    vv = c.incd("v", lambda eng: eng.tensor_scalar(out=m[0:R, 0:2], in0=m[0:R, 0:2],
                                                  scalar1=1.0 / D, scalar2=None, op0=ALU.mult))
    c.wait("s", "V", vv)
    sv = c.incd("s", lambda eng: eng.activation(out=rstd[0:R, 0:1], in_=m[0:R, 0:1],
                                               func=AF.Square))
    c.wait("v", "S", sv)
    vv = c.incd("v", lambda eng: eng.tensor_sub(out=rstd[0:R, 0:1], in0=m[0:R, 1:2],
                                               in1=rstd[0:R, 0:1]))
    c.wait("s", "V", vv)
    sv = c.incd("s", lambda eng: eng.activation(out=rstd[0:R, 0:1], in_=rstd[0:R, 0:1],
                                               func=AF.Sqrt, bias=h.kv[0:R, 0:1]))
    c.wait("v", "S", sv)
    c.inc("v", lambda eng: eng.reciprocal(out=rstd[0:R, 1:2], in_=rstd[0:R, 0:1]))
    vv = c.incd("v", lambda eng: eng.tensor_scalar(out=xn_tm[0:R, :], in0=x_ap,
                                                  scalar1=m[0:R, 0:1], scalar2=rstd[0:R, 1:2],
                                                  op0=ALU.subtract, op1=ALU.mult))
    c.wait("t", "V", vv)
    tv = None
    for j in range(8):
        tv = c.inc("t", lambda eng, i=xn_tm[0:R, j * 128:(j + 1) * 128],
                   o=h.psA[:, j * 128:j * 128 + R]:
                   eng.transpose(out=o, in_=i, identity=h.id32[0:R, 0:R]))
    c.wait("v", "T", tv)
    for j in range(8):
        c.inc("v", lambda eng, o=xn_dst_fn(j), i=h.psA[:, j * 128:j * 128 + R]:
              eng.tensor_copy(out=o, in_=i))
    vv = c.cnt["V"]
    c.wait("t", "V", vv)
    return vv


def _mk_scr(c, es):
    nc = c.nc
    scr = {
        "m": es.enter_context(_sbt(nc, "scr_m", [128, 4], F32)),
        "sq": es.enter_context(_sbt(nc, "scr_sq", [128, 1024], F32)),
        "rstd": es.enter_context(_sbt(nc, "scr_rstd", [128, 2], F32)),
    }
    scr["xn_tm"] = scr["sq"]
    return scr


def emit_mamba(c, h, es, io, scr, full, dbg=False, dbg_tile=(0, 0), state_dst=None):
    MGS, MGT, MNG = 256, 2, 8
    nc = c.nc
    wcol0 = 0 if full else 2048
    wcols = RIN - wcol0
    W = emit_load_wchunks(c, es, io, "W_inT", 1024, RIN, tag="win") if full else \
        emit_load_wchunks_sub(c, es, io, "W_inT", 1024, RIN, wcol0, tag="win")
    if full:
        Wout = emit_load_wchunks(c, es, io, "W_outT", DI, 1024, tag="wout")
    if full:
        wcols = RIN

    xg = es.enter_context(_sbt(nc, "m_xng", [128, 8 * MGS], BF16))
    xfm = es.enter_context(_sbt(nc, "m_xfm", [128, 18 * MGS], BF16))
    slab = es.enter_context(_sbt(nc, "m_slab", [128, MGS + 3], F32))
    tails = es.enter_context(_sbt(nc, "m_tails", [128, 54], F32))
    xtile2h = es.enter_context(_sbt(nc, "m_xtile", [128, 2048], F32))
    xtile2 = [xtile2h[:, 0:1024], xtile2h[:, 1024:2048]]
    xtile = xtile2[0]
    dtv = es.enter_context(_sbt(nc, "m_dtv", [128, 32], F32))
    dts1 = es.enter_context(_sbt(nc, "m_dts1", [128, 32], F32))
    dtT = es.enter_context(_sbt(nc, "m_dtT", [128, 32], F32))
    a_T = es.enter_context(_sbt(nc, "m_aT", [128, 32], F32))
    AcumT = es.enter_context(_sbt(nc, "m_AcumT", [128, 32], F32))
    Trow = es.enter_context(_sbt(nc, "m_Trow", [1, 32], F32))
    w_T = es.enter_context(_sbt(nc, "m_wT", [128, 32], F32))
    eT_b = es.enter_context(_sbt(nc, "m_eTbc", [128, 32], F32))
    dtw = es.enter_context(_sbt(nc, "m_dtw", [128, 32], F32))
    xdw = es.enter_context(_sbt(nc, "m_xdw", [128, DI], BF16))
    B_tm = es.enter_context(_sbt(nc, "m_Btm", [128, 128], BF16))
    S_a = es.enter_context(_sbt(nc, "m_Sa", [128, DI], F32))
    if full:
        u_T = es.enter_context(_sbt(nc, "m_uT", [128, 32], F32))
        xd = es.enter_context(_sbt(nc, "m_xd", [128, DI], BF16))
        amq = es.enter_context(_sbt(nc, "m_amq", [128, 1024], F32))
        Mtq = es.enter_context(_sbt(nc, "m_Mtq", [128, 1024], BF16))
        CBm = es.enter_context(_sbt(nc, "m_CBm", [128, 128], BF16))
        S_bf = es.enter_context(_sbt(nc, "m_Sbf", [128, DI], BF16))
        siluz = es.enter_context(_sbt(nc, "m_siluz", [128, DI], BF16))
        t1 = es.enter_context(_sbt(nc, "m_t1", [128, DI], F32))
        t3 = es.enter_context(_sbt(nc, "m_t3", [128, DI], BF16))
        yn = es.enter_context(_sbt(nc, "m_yn", [128, DI], BF16))
        y_fm = xdw
        xout = scr["sq"]
        S_b = t1
    else:
        S_b = es.enter_context(_sbt(nc, "m_Sb", [128, DI], F32))

    if full:
        lv = c.inc("y", lambda eng: eng.dma_start(out=S_a[:], in_=io["S_in"]))
        c.wait("v", "ld", lv)
        if "smask_sb" in io:
            c.inc("v", lambda eng: eng.tensor_scalar(
                out=S_a[:], in0=S_a[:], scalar1=io["smask_sb"][:, 0:1], scalar2=None,
                op0=ALU.mult))
        vv = c.incd("v", lambda eng: eng.tensor_copy(out=S_bf[:], in_=S_a[:]))
        c.wait("t", "V", vv)
    else:
        vv = c.inc("v", lambda eng: eng.memset(S_a[:], 0.0))
        c.wait("t", "V", vv)

    es_halo = contextlib.ExitStack()
    uhn128 = es_halo.enter_context(_sbt(nc, "m_uhn128", [128, 1024], BF16, side="right"))
    vv = c.inc("v", lambda eng: eng.memset(xtile, 0.0))
    c.wait("y", "V", vv)
    lv = c.inc("y", lambda eng: eng.dma_start(out=xtile2h[0:3, 0:1024], in_=io["uh"]))
    c.wait("v", "ld", lv)
    c.wait("s", "ld", lv)
    emit_ln_rows(c, h, xtile, 128, scr,
                 lambda j: uhn128[:, j * 128:(j + 1) * 128])
    for rc in range(18):
        wc = (2048 - wcol0) + rc * 128
        for dc in range(8):
            c.inc("t", lambda eng, l=W[:, dc * wcols + wc: dc * wcols + wc + 128],
                  r=uhn128[:, dc * 128: dc * 128 + 3], o=h.psB[:, rc * 4: rc * 4 + 3],
                  st=(dc == 0), sp=(dc == 7):
                  eng.matmul(out=o, lhsT=l, rhs=r, start=st, stop=sp))
    tv = c.cnt["T"]
    c.wait("v", "T", tv)
    vv = c.inc("v", lambda eng: eng.tensor_copy(
        out=tails[:].rearrange("p (r k) -> p r k", k=3),
        in_=h.psB[:, 0:72].rearrange("p (r k) -> p r k", k=4)[:, :, 0:3]))
    c.wait("t", "V", vv)
    es_halo.close()

    u_t = io["u"].rearrange("(n p) d -> n p d", p=128)
    x1_t = io["x1"].rearrange("(n p) d -> n p d", p=128) if full else None

    for g in range(MNG):
        for k in range(MGT):
            ti = g * MGT + k
            buf = xtile2[k % 2]
            c.wait("y", "V", c.cnt["V"])
            c.wait("y", "S", c.cnt["S"])
            c.wait("s", "st", c.cnt["st"])
            lv = c.inc("y", lambda eng, s=u_t[ti], d=buf: eng.dma_start(out=d, in_=s))
            c.wait("v", "ld", lv)
            c.wait("s", "ld", lv)
            emit_ln_rows(c, h, buf, 128, scr,
                         lambda j, k=k: xg[:, j * MGS + k * 128: j * MGS + (k + 1) * 128])
        for rc in range(18):
            wc = (2048 - wcol0) + rc * 128
            c.wait("t", "V", c.cnt["V"])
            for dc in range(8):
                c.inc("t", lambda eng, l=W[:, dc * wcols + wc: dc * wcols + wc + 128],
                      r=xg[:, dc * MGS:(dc + 1) * MGS], o=h.psB[:, 0:MGS],
                      st=(dc == 0), sp=(dc == 7):
                      eng.matmul(out=o, lhsT=l, rhs=r, start=st, stop=sp))
            tv = c.cnt["T"]
            c.wait("v", "T", tv)
            c.inc("v", lambda eng, rc=rc: eng.tensor_copy(out=slab[:, 0:3],
                                                          in_=tails[:, rc * 3:(rc + 1) * 3]))
            vv = c.inc("v", lambda eng: eng.tensor_copy(out=slab[:, 3:3 + MGS], in_=h.psB[:, 0:MGS]))
            c.wait("t", "V", vv)
            c.inc("v", lambda eng, rc=rc: eng.tensor_copy(out=tails[:, rc * 3:(rc + 1) * 3],
                                                          in_=slab[:, MGS:MGS + 3]))
            c.inc("v", lambda eng, rc=rc: eng.tensor_scalar(
                out=scr["sq"][:, 0:MGS], in0=slab[:, 0:MGS],
                scalar1=h.convw[:, rc * 4:rc * 4 + 1], scalar2=None, op0=ALU.mult))
            for kk in range(1, 4):
                c.inc("v", lambda eng, rc=rc, kk=kk: eng.scalar_tensor_tensor(
                    out=scr["sq"][:, 0:MGS], in0=slab[:, kk:kk + MGS],
                    scalar=h.convw[:, rc * 4 + kk:rc * 4 + kk + 1], in1=scr["sq"][:, 0:MGS],
                    op0=ALU.mult, op1=ALU.add))
            vv = c.cnt["V"]
            c.wait("s", "V", vv)
            sv = c.inc("s", lambda eng, rc=rc: eng.activation(
                out=xfm[:, rc * MGS:(rc + 1) * MGS], in_=scr["sq"][:, 0:MGS], func=AF.Silu,
                bias=h.convb[:, rc:rc + 1]))
            c.wait("v", "S", sv)

        svx = c.cnt["S"]
        c.wait("t", "S", svx)

        for k in range(MGT):
            ti = g * MGT + k
            tcol = k * 128

            wc = 4352 - wcol0
            c.wait("t", "V", c.cnt["V"])
            for dc in range(8):
                c.inc("t", lambda eng, l=xg[:, dc * MGS + tcol: dc * MGS + tcol + 128],
                      r=W[:, dc * wcols + wc: dc * wcols + wc + 32], o=h.psB[:, 0:32],
                      st=(dc == 0), sp=(dc == 7):
                      eng.matmul(out=o, lhsT=l, rhs=r, start=st, stop=sp))
            tv = c.cnt["T"]
            c.wait("v", "T", tv)
            vv = c.incd("v", lambda eng: eng.tensor_add(out=dtv[:], in0=h.psB[:, 0:32],
                                                       in1=h.dtb_b[:]))
            c.wait("s", "V", vv)
            c.inc("s", lambda eng: eng.activation(out=dts1[:], in_=dtv[:], func=AF.Relu))
            c.inc("s", lambda eng: eng.activation(out=dtT[:], in_=dtv[:], func=AF.Abs))
            c.inc("s", lambda eng: eng.activation(out=dtT[:], in_=dtT[:], func=AF.Exp, scale=-1.0))
            sv = c.incd("s", lambda eng: eng.activation(out=dtT[:], in_=dtT[:], func=AF.Ln, bias=h.kv[:, 1:2]))
            c.wait("v", "S", sv)
            c.inc("v", lambda eng: eng.tensor_add(out=dtT[:], in0=dts1[:], in1=dtT[:]))
            vv = c.incd("v", lambda eng: eng.tensor_mul(out=a_T[:], in0=dtT[:], in1=h.Aneg_b[:]))
            c.wait("t", "V", vv)

            c.inc("t", lambda eng: eng.matmul(out=h.psB[:, 64:96], lhsT=h.U128[:], rhs=a_T[:],
                                              start=True, stop=True))
            tvT = c.inc("t", lambda eng: eng.matmul(out=h.psB[0:1, 128:160], lhsT=h.ones_col[:],
                                                    rhs=a_T[:], start=True, stop=True))
            c.wait("v", "T", tvT)
            c.wait("s", "T", tvT)
            vv = c.incd("v", lambda eng: eng.tensor_copy(out=AcumT[:], in_=h.psB[:, 64:96]))
            if full:
                c.incd("s", lambda eng: eng.activation(out=u_T[:], in_=h.psB[:, 64:96], func=AF.Exp))
            sv = c.incd("s", lambda eng: eng.copy(out=Trow[:], in_=h.psB[0:1, 128:160]))
            c.wait("t", "S", sv)
            c.wait("t", "V", vv)
            tvB = c.inc("t", lambda eng: eng.matmul(out=h.psB[:, 192:224], lhsT=h.ones_row[:],
                                                    rhs=Trow[:], start=True, stop=True))
            c.wait("v", "T", tvB)
            c.wait("s", "T", tvB)
            vv = c.incd("v", lambda eng: eng.tensor_sub(out=w_T[:], in0=h.psB[:, 192:224],
                                                       in1=AcumT[:]))
            sv1 = c.incd("s", lambda eng: eng.activation(out=eT_b[:], in_=h.psB[:, 192:224],
                                                        func=AF.Exp))
            c.wait("s", "V", vv)
            sv = c.incd("s", lambda eng: eng.activation(out=w_T[:], in_=w_T[:], func=AF.Exp))
            c.wait("v", "S", sv)
            c.wait("t", "S", sv)
            c.inc("v", lambda eng: eng.tensor_mul(out=dtw[:], in0=dtT[:], in1=w_T[:]))

            for rnd in range(2):
                c.wait("t", "V", c.cnt["V"])
                for jj in range(8):
                    j = rnd * 8 + jj
                    c.inc("t", lambda eng, i=xfm[:, j * MGS + tcol: j * MGS + tcol + 128],
                          o=h.psTb[:, jj * 128:(jj + 1) * 128]:
                          eng.transpose(out=o, in_=i, identity=h.idbf[:]))
                tv = c.cnt["T"]
                c.wait("v", "T", tv)
                ps3 = h.psTb[:, 0:1024].rearrange("p (hh q) -> p hh q", q=64)
                vv = c.incd("v", lambda eng, rnd=rnd, ps3=ps3: eng.tensor_tensor(
                    out=xdw[:, rnd * 1024:(rnd + 1) * 1024].rearrange(
                        "p (hh q) -> p hh q", q=64),
                    in0=ps3, in1=bcast_ap(dtw[:, rnd * 16:(rnd + 1) * 16], 16, 64),
                    op=ALU.mult))
                if full:
                    vv = c.incd("v", lambda eng, rnd=rnd, ps3=ps3: eng.tensor_tensor(
                        out=xd[:, rnd * 1024:(rnd + 1) * 1024].rearrange(
                            "p (hh q) -> p hh q", q=64),
                        in0=ps3, in1=bcast_ap(dtT[:, rnd * 16:(rnd + 1) * 16], 16, 64),
                        op=ALU.mult))
                c.wait("t", "V", vv)

            tv = c.inc("t", lambda eng, i=xfm[:, 16 * MGS + tcol: 16 * MGS + tcol + 128]:
                       eng.transpose(out=h.psTb[:, 0:128], in_=i, identity=h.idbf[:]))
            c.wait("v", "T", tv)
            vv = c.incd("v", lambda eng: eng.tensor_copy(out=B_tm[:], in_=h.psTb[:, 0:128]))
            c.wait("t", "V", vv)

            if full:
                tv = c.inc("t", lambda eng,
                           l=xfm[:, 16 * MGS + tcol: 16 * MGS + tcol + 128],
                           r=xfm[:, 17 * MGS + tcol: 17 * MGS + tcol + 128]:
                           eng.matmul(out=h.psB[:, 0:128], lhsT=l, rhs=r,
                                      start=True, stop=True))
                c.wait("v", "T", tv)
                vv = c.incd("v", lambda eng: eng.tensor_mul(out=CBm[:], in0=h.psB[:, 0:128],
                                                           in1=h.TRIU32[:]))
                c.wait("t", "V", vv)

                for seg in range(4):
                    wc = seg * 512 - wcol0
                    for dc in range(8):
                        c.inc("t", lambda eng, l=xg[:, dc * MGS + tcol: dc * MGS + tcol + 128],
                              r=W[:, dc * wcols + wc: dc * wcols + wc + 512],
                              o=h.psB[:, seg * 512:(seg + 1) * 512],
                              st=(dc == 0), sp=(dc == 7):
                              eng.matmul(out=o, lhsT=l, rhs=r, start=st, stop=sp))
                tv = c.cnt["T"]
                c.wait("s", "T", tv)
                sv = c.inc("s", lambda eng: eng.activation(out=siluz[:], in_=h.psB[:, 0:DI],
                                                           func=AF.Silu))
                c.wait("t", "S", sv)
                c.wait("v", "S", sv)

                for hh in range(32):
                    c.inc("t", lambda eng, hh=hh,
                          l=xfm[:, 17 * MGS + tcol: 17 * MGS + tcol + 128]: eng.matmul(
                        out=h.psB[:, hh * 64:(hh + 1) * 64], lhsT=l,
                        rhs=S_bf[:, hh * 64:(hh + 1) * 64], start=True, stop=True))
                tv = c.cnt["T"]
                c.wait("v", "T", tv)
                vv = c.inc("v", lambda eng: eng.tensor_tensor(
                    out=t1[:].rearrange("p (hh q) -> p hh q", q=64),
                    in0=h.psB[:, 0:DI].rearrange("p (hh q) -> p hh q", q=64),
                    in1=bcast_ap(u_T[:], 32, 64), op=ALU.mult))
                c.wait("t", "V", vv)
                for qq in range(4):
                    h0 = qq * 8
                    vv = c.incd("v", lambda eng, h0=h0: eng.tensor_tensor(
                        out=amq[:].rearrange("p (hh l) -> p hh l", l=128),
                        in0=bcast_ap(a_T[:, h0:h0 + 8], 8, 128),
                        in1=h.TRIU32[:].unsqueeze(1).broadcast_to([128, 8, 128]),
                        op=ALU.mult))
                    c.wait("t", "V", vv)
                    for q2 in range(2):
                        c.inc("t", lambda eng, q2=q2: eng.matmul(
                            out=h.psA[:, q2 * 512:(q2 + 1) * 512], lhsT=h.SLT32[:],
                            rhs=amq[:, q2 * 512:(q2 + 1) * 512], start=True, stop=True))
                    tv = c.cnt["T"]
                    c.wait("s", "T", tv)
                    sv = c.inc("s", lambda eng: eng.activation(out=Mtq[:], in_=h.psA[:, 0:1024],
                                                               func=AF.Exp))
                    c.wait("v", "S", sv)
                    c.wait("t", "S", sv)
                    vv = c.incd("v", lambda eng: eng.tensor_tensor(
                        out=Mtq[:].rearrange("p (hh l) -> p hh l", l=128),
                        in0=Mtq[:].rearrange("p (hh l) -> p hh l", l=128),
                        in1=CBm[:].unsqueeze(1).broadcast_to([128, 8, 128]), op=ALU.mult))
                    c.wait("t", "V", vv)
                    for hh in range(h0, h0 + 8):
                        c.inc("t", lambda eng, hh=hh, h0=h0: eng.matmul(
                            out=h.psB[:, hh * 64:(hh + 1) * 64],
                            lhsT=Mtq[:, (hh - h0) * 128:(hh - h0 + 1) * 128],
                            rhs=xd[:, hh * 64:(hh + 1) * 64], start=True, stop=True))
                tv = c.cnt["T"]
                c.inc("v", lambda eng: eng.reciprocal(out=dts1[:], in_=dtT[:]))
                c.inc("v", lambda eng: eng.tensor_mul(out=dts1[:], in0=dts1[:], in1=h.D_b[:]))
                c.inc("v", lambda eng: eng.tensor_tensor(
                    out=t3[:].rearrange("p (hh q) -> p hh q", q=64),
                    in0=xd[:].rearrange("p (hh q) -> p hh q", q=64),
                    in1=bcast_ap(dts1[:], 32, 64), op=ALU.mult))
                c.inc("v", lambda eng: eng.tensor_add(out=t1[:], in0=t1[:], in1=t3[:]))
                c.wait("v", "T", tv)
                vv = c.inc("v", lambda eng: eng.tensor_add(out=t1[:], in0=h.psB[:, 0:DI],
                                                           in1=t1[:]))
                c.wait("t", "V", vv)

            for hh in range(32):
                c.inc("t", lambda eng, hh=hh: eng.matmul(
                    out=h.psB[:, hh * 64:(hh + 1) * 64], lhsT=B_tm[:],
                    rhs=xdw[:, hh * 64:(hh + 1) * 64], start=True, stop=True))
            tvG = c.cnt["T"]

            if full:
                vv = c.inc("v", lambda eng: eng.tensor_mul(out=t1[:], in0=t1[:], in1=siluz[:]))
                c.wait("s", "V", vv)
                sv = c.inc("s", lambda eng: eng.activation(out=t3[:], in_=t1[:], func=AF.Square,
                                                           accum_out=scr["m"][:, 2:3]))
                c.inc("s", lambda eng: eng.activation(out=scr["m"][:, 3:4], in_=scr["m"][:, 2:3],
                                                      func=AF.Ln, bias=h.kv[:, 0:1], scale=1.0 / DI))
                sv = c.incd("s", lambda eng: eng.activation(out=scr["m"][:, 3:4], in_=scr["m"][:, 3:4],
                                                           func=AF.Exp, scale=-0.5))
                c.wait("v", "S", sv)
                vv = c.incd("v", lambda eng: eng.scalar_tensor_tensor(
                    out=yn[:], in0=t1[:], scalar=scr["m"][:, 3:4], in1=h.normw_b[:],
                    op0=ALU.mult, op1=ALU.mult))

            c.wait("v", "T", tvG)
            c.inc("v", lambda eng: eng.tensor_tensor(
                out=S_b[:].rearrange("p (hh q) -> p hh q", q=64),
                in0=S_a[:].rearrange("p (hh q) -> p hh q", q=64),
                in1=bcast_ap(eT_b[:], 32, 64), op=ALU.mult))
            vv = c.inc("v", lambda eng: eng.tensor_add(out=S_a[:], in0=S_b[:], in1=h.psB[:, 0:DI]))
            c.wait("t", "V", vv)
            if dbg and (g, k) == dbg_tile:
                c.barrier()
                dlist = [("dbg_tails", tails),
                         ("dbg_xg", xg), ("dbg_xfm", xfm), ("dbg_dtT", dtT),
                         ("dbg_aT", a_T), ("dbg_AcumT", AcumT), ("dbg_wT", w_T),
                         ("dbg_eTb", eT_b), ("dbg_xdw", xdw), ("dbg_Btm", B_tm),
                         ("dbg_S", S_a)]
                if full:
                    dlist += [("dbg_xd", xd), ("dbg_uT", u_T), ("dbg_siluz", siluz),
                              ("dbg_t1", t1), ("dbg_yn", yn), ("dbg_CBm", CBm),
                              ("dbg_Db", h.D_b), ("dbg_normwb", h.normw_b),
                              ("dbg_rms", scr["m"]), ("dbg_Mtq", Mtq), ("dbg_amq", amq)]
                for nm, buf in dlist:
                    c.inc("g", lambda eng, d=io[nm], s=buf[:]: eng.dma_start(out=d, in_=s),
                          sname="st")
                c.barrier()
            if full:
                vv = c.incd("v", lambda eng: eng.tensor_copy(out=S_bf[:], in_=S_a[:]))

            if full:
                for rnd in range(2):
                    c.wait("t", "V", c.cnt["V"])
                    for jj in range(8):
                        j = rnd * 8 + jj
                        c.inc("t", lambda eng, i=yn[:, j * 128:(j + 1) * 128],
                              o=h.psTb[:, jj * 128:(jj + 1) * 128]:
                              eng.transpose(out=o, in_=i, identity=h.idbf[:]))
                    tv = c.cnt["T"]
                    c.wait("v", "T", tv)
                    c.inc("v", lambda eng, rnd=rnd: eng.tensor_copy(
                        out=y_fm[:, rnd * 1024:(rnd + 1) * 1024], in_=h.psTb[:, 0:1024]))
                vv = c.cnt["V"]
                c.wait("t", "V", vv)
                lv = c.inc("y", lambda eng, s=u_t[ti], d=xtile2[k % 2]:
                           eng.dma_start(out=d, in_=s))
                for seg in range(2):
                    for kc in range(16):
                        c.inc("t", lambda eng, kc=kc, seg=seg, st=(kc == 0), sp=(kc == 15):
                              eng.matmul(
                                  out=h.psB[:, seg * 512:(seg + 1) * 512],
                                  lhsT=y_fm[:, kc * 128:(kc + 1) * 128],
                                  rhs=Wout[:, kc * 1024 + seg * 512: kc * 1024 + (seg + 1) * 512],
                                  start=st, stop=sp))
                tv = c.cnt["T"]
                c.wait("v", "T", tv)
                c.wait("v", "ld", lv)
                vv = c.incd("v", lambda eng, i0=xtile2[k % 2]: eng.tensor_add(
                    out=xout[:, 0:1024], in0=i0, in1=h.psB[:, 0:1024]))
                c.wait("t", "V", vv)
                c.wait("g", "V", vv)
                stv = c.inc("g", lambda eng, d=x1_t[ti]: eng.dma_start(
                    out=d, in_=xout[:, 0:1024]), sname="st")
                c.wait("v", "st", stv)
                c.wait("y", "st", stv)

    if not full:
        c.barrier()
        sdst = state_dst if state_dst is not None else io["S_out"]
        stv = c.inc("g", lambda eng, d=sdst: eng.dma_start(out=d, in_=S_a[:]), sname="st")
        c.wait("g", "st", stv)


def emit_load_wchunks_sub(c, es, io, name, rows, cols, col0, tag=None, colseg=1096):
    nc = c.nc
    nch = rows // 128
    ncols = cols - col0
    tag = tag or name
    dst = es.enter_context(_sbt(nc, f"w_{tag}", [128, nch * ncols], BF16))
    segs = [(s, min(s + colseg, ncols)) for s in range(0, ncols, colseg)]
    with _sbt(nc, f"stg_{tag}", [128, segs[0][1]], F32, side="right") as stage:
        src = io[name].rearrange("(ch p) d -> ch p d", p=128)
        for ch in range(nch):
            for (s0, s1) in segs:
                c.wait("y", "G", c.cnt["G"])
                lv = c.inc("y", lambda eng, s=src[ch][:, col0 + s0: col0 + s1],
                           d=stage[:, 0:s1 - s0]: eng.dma_start(out=d, in_=s))
                c.wait("g", "ld", lv)
                c.inc("g", lambda eng, d=dst[:, ch * ncols + s0: ch * ncols + s1],
                      s=stage[:, 0:s1 - s0]: eng.tensor_copy(out=d, in_=s))
    gv = c.cnt["G"]
    for e in ("t", "v", "s"):
        c.wait(e, "G", gv)
    return dst


def emit_moe(c, h, es, io, scr):
    nc = c.nc
    xg = es.enter_context(_sbt(nc, "e_xng", [128, 8 * GS], BF16))
    xtile = es.enter_context(_sbt(nc, "e_xtile", [128, 1024], F32))
    gwT = es.enter_context(_sbt(nc, "e_gwT", [128, 64], BF16))
    gwst = es.enter_context(_sbt(nc, "e_gwst", [128, 64], F32, side="right"))
    b1f = es.enter_context(_sbt(nc, "e_b1f", [128, 72], F32))
    b3f = es.enter_context(_sbt(nc, "e_b3f", [128, 72], F32))
    b2sb = es.enter_context(_sbt(nc, "e_b2sb", [8, 1024], BF16))
    b2st = es.enter_context(_sbt(nc, "e_b2st", [8, 1024], F32, side="right"))
    probs = es.enter_context(_sbt(nc, "e_probs", [128, 8], F32))
    ltb = es.enter_context(_sbt(nc, "e_ltb", [128, 8], F32))
    rm = es.enter_context(_sbt(nc, "e_rm", [128, 4], F32))
    comb = es.enter_context(_sbt(nc, "e_comb", [128, 4 * 32], F32))
    combT = es.enter_context(_sbt(nc, "e_combT", [32, 4 * 128], BF16))
    h1s = es.enter_context(_sbt(nc, "e_h1s", [128, 8 * GS], BF16))
    h3b = es.enter_context(_sbt(nc, "e_h3b", [128, 8 * GS], BF16))
    y_acc = es.enter_context(_sbt(nc, "e_yacc", [128, 4 * 1024], F32))
    Wbuf = [es.enter_context(_sbt(nc, f"e_W{b}", [128, 3 * 8192], BF16)) for b in range(2)]
    stage = es.enter_context(_sbt(nc, "e_stg", [128, 4096], F32, side="right"))

    lv = c.inc("y", lambda eng: eng.dma_start(
        out=b1f[:, 0:64].rearrange("p (e i) -> p e i", i=8),
        in_=io["moe_b1_fm"].rearrange("e p i -> p e i")))
    lv = c.inc("y", lambda eng: eng.dma_start(
        out=b3f[:, 0:64].rearrange("p (e i) -> p e i", i=8),
        in_=io["moe_b3_fm"].rearrange("e p i -> p e i")))
    lv = c.inc("y", lambda eng: eng.dma_start(out=b1f[:, 64:72], in_=io["sb1_fm"]))
    lv = c.inc("y", lambda eng: eng.dma_start(out=b3f[:, 64:72], in_=io["sb3_fm"]))
    lv = c.inc("y", lambda eng: eng.dma_start(out=b2st[:], in_=io["moe_b2"]))
    lv = c.inc("y", lambda eng: eng.dma_start(
        out=gwst[:].rearrange("p (ch i) -> p ch i", i=8),
        in_=io["gate_wT"].rearrange("(ch p) i -> p ch i", p=128)))
    c.wait("g", "ld", lv)
    c.inc("g", lambda eng: eng.tensor_copy(out=b2sb[:], in_=b2st[:]))
    gv = c.inc("g", lambda eng: eng.tensor_copy(out=gwT[:], in_=gwst[:]))
    for e in ("t", "v", "s"):
        c.wait(e, "G", gv)
    vv = c.inc("v", lambda eng: eng.memset(comb[:], 0.0))
    vv = c.inc("v", lambda eng: eng.memset(combT[:], 0.0))
    c.wait("t", "V", vv)

    def w_names(e):
        if e < NE:
            return [("moe_w1", e), ("moe_w3", e), ("moe_w2", e)]
        return [("sw1", None), ("sw3", None), ("sw2", None)]

    def load_expert(e, guardT):
        buf = Wbuf[e % 2]
        for mi, (nm, ei) in enumerate(w_names(e)):
            src = io[nm]
            if ei is not None:
                src = src[ei]
            src = src.rearrange("(ch p) d -> p ch d", p=128)
            for hf in range(2):
                c.wait("y", "G", c.cnt["G"])
                c.wait("y", "T", guardT)
                lv = c.inc("y", lambda eng, s=src[:, hf * 4:(hf + 1) * 4, :]: eng.dma_start(
                    out=stage[:].rearrange("p (ch d) -> p ch d", d=1024), in_=s))
                c.wait("g", "ld", lv)
                c.inc("g", lambda eng, d=buf[:, mi * 8192 + hf * 4096:
                                             mi * 8192 + (hf + 1) * 4096]:
                      eng.tensor_copy(out=d, in_=stage[:]))
        return c.cnt["G"]

    u_t = io["u"].rearrange("(n p) d -> n p d", p=128)
    x1_t = io["x1"].rearrange("(n p) d -> n p d", p=128)
    wready = [0, 0]
    wguardT = [0, 0]
    wready[0] = load_expert(0, 0)

    for g in range(NG):
        for k in range(4):
            ti = g * 4 + k
            c.wait("y", "V", c.cnt["V"])
            c.wait("y", "S", c.cnt["S"])
            lv = c.inc("y", lambda eng, s=u_t[ti]: eng.dma_start(out=xtile[:], in_=s))
            c.wait("v", "ld", lv)
            c.wait("s", "ld", lv)
            emit_ln_rows(c, h, xtile[:], 128, scr,
                         lambda j, k=k: xg[:, j * GS + k * 128: j * GS + (k + 1) * 128])
        for k in range(4):
            c.wait("t", "V", c.cnt["V"])
            for dc in range(8):
                c.inc("t", lambda eng, l=xg[:, dc * GS + k * 128: dc * GS + (k + 1) * 128],
                      r=gwT[:, dc * 8:(dc + 1) * 8], o=h.psB[:, 0:8],
                      st=(dc == 0), sp=(dc == 7):
                      eng.matmul(out=o, lhsT=l, rhs=r, start=st, stop=sp))
            tv = c.cnt["T"]
            c.wait("v", "T", tv)
            c.inc("v", lambda eng: eng.reduce_max(out=rm[:, 0:1], in_=h.psB[:, 0:8], axis=AX.X))
            vv = c.inc("v", lambda eng: eng.tensor_scalar(out=rm[:, 1:2], in0=rm[:, 0:1],
                                                          scalar1=-1.0, scalar2=None, op0=ALU.mult))
            c.wait("s", "V", vv)
            sv = c.inc("s", lambda eng: eng.activation(out=probs[:], in_=h.psB[:, 0:8],
                                                       func=AF.Exp, bias=rm[:, 1:2]))
            c.wait("v", "S", sv)
            c.wait("t", "S", sv)
            c.inc("v", lambda eng: eng.reduce_sum(out=rm[:, 2:3], in_=probs[:], axis=AX.X))
            c.inc("v", lambda eng: eng.reciprocal(out=rm[:, 3:4], in_=rm[:, 2:3]))
            c.inc("v", lambda eng: eng.tensor_scalar(out=probs[:], in0=probs[:],
                                                     scalar1=rm[:, 3:4], scalar2=None,
                                                     op0=ALU.mult))
            c.inc("v", lambda eng: eng.reduce_max(out=rm[:, 0:1], in_=probs[:], axis=AX.X))
            c.inc("v", lambda eng: eng.tensor_scalar(out=ltb[:], in0=probs[:], scalar1=rm[:, 0:1],
                                                     scalar2=None, op0=ALU.is_lt))
            c.inc("v", lambda eng: eng.tensor_scalar(out=ltb[:], in0=ltb[:], scalar1=1000.0,
                                                     scalar2=-1000.0, op0=ALU.mult, op1=ALU.add))
            c.inc("v", lambda eng: eng.tensor_add(out=ltb[:], in0=ltb[:], in1=probs[:]))
            c.inc("v", lambda eng: eng.reduce_max(out=rm[:, 1:2], in_=ltb[:], axis=AX.X))
            c.inc("v", lambda eng: eng.tensor_scalar(out=ltb[:], in0=probs[:], scalar1=rm[:, 1:2],
                                                     scalar2=None, op0=ALU.is_lt))
            c.inc("v", lambda eng: eng.tensor_scalar(out=ltb[:], in0=ltb[:], scalar1=-1.0,
                                                     scalar2=1.0, op0=ALU.mult, op1=ALU.add))
            vv = c.inc("v", lambda eng, k=k: eng.tensor_mul(out=comb[:, k * 32:k * 32 + 8],
                                                            in0=probs[:], in1=ltb[:]))
            c.wait("t", "V", vv)
            tv = c.inc("t", lambda eng, k=k: eng.transpose(
                out=h.psA[0:32, 0:128], in_=comb[:, k * 32:(k + 1) * 32], identity=h.id32[:]))
            c.wait("v", "T", tv)
            vv = c.inc("v", lambda eng, k=k: eng.tensor_copy(
                out=combT[:, k * 128:(k + 1) * 128], in_=h.psA[0:32, 0:128]))
            c.wait("t", "V", vv)

        vv = c.inc("v", lambda eng: eng.memset(y_acc[:], 0.0))
        c.wait("t", "V", vv)
        c.wait("t", "S", c.cnt["S"])

        for e in range(NE + 1):
            gv = wready[e % 2]
            for en in ("t",):
                c.wait(en, "G", gv)
            buf = Wbuf[e % 2]
            bcol = 64 if e == NE else e * 8
            s_base = c.cnt["S"]
            v_base = None
            for m_i, (dstbuf, func, bias) in enumerate(
                    ((h1s, AF.Silu, b1f), (h3b, AF.Copy, b3f))):
                base = m_i * 8192
                if m_i == 1:
                    v_base = c.cnt["V"]
                for ic in range(8):
                    gidx = m_i * 8 + ic
                    seg = gidx % 4
                    if gidx >= 4:
                        if gidx - 4 < 8:
                            c.wait("t", "S", s_base + (gidx - 4) + 1)
                        else:
                            c.wait("t", "V", v_base + (gidx - 4 - 8) + 1)
                    for dc in range(8):
                        c.inc("t", lambda eng, l=buf[:, base + dc * 1024 + ic * 128:
                                                     base + dc * 1024 + ic * 128 + 128],
                              r=xg[:, dc * GS:(dc + 1) * GS],
                              o=h.psB[:, seg * 512:(seg + 1) * 512],
                              st=(dc == 0), sp=(dc == 7):
                              eng.matmul(out=o, lhsT=l, rhs=r, start=st, stop=sp))
                    tv = c.cnt["T"]
                    if func is AF.Silu:
                        c.wait("s", "T", tv)
                        c.inc("s", lambda eng, ic=ic, dstbuf=dstbuf, bias=bias,
                              seg=seg, bcol=bcol:
                              eng.activation(out=dstbuf[:, ic * GS:(ic + 1) * GS],
                                             in_=h.psB[:, seg * 512:(seg + 1) * 512],
                                             func=AF.Silu,
                                             bias=bias[:, bcol + ic:bcol + ic + 1]))
                    else:
                        c.wait("v", "T", tv)
                        c.inc("v", lambda eng, ic=ic, dstbuf=dstbuf, bias=bias,
                              seg=seg, bcol=bcol:
                              eng.tensor_scalar(out=dstbuf[:, ic * GS:(ic + 1) * GS],
                                                in0=h.psB[:, seg * 512:(seg + 1) * 512],
                                                scalar1=bias[:, bcol + ic:bcol + ic + 1],
                                                scalar2=None, op0=ALU.add))
            sv = c.cnt["S"]
            c.wait("t", "S", sv)
            c.wait("v", "S", sv)
            for ic in range(8):
                c.inc("v", lambda eng, ic=ic: eng.tensor_mul(
                    out=h1s[:, ic * GS:(ic + 1) * GS], in0=h1s[:, ic * GS:(ic + 1) * GS],
                    in1=h3b[:, ic * GS:(ic + 1) * GS]))
            vv = c.cnt["V"]
            c.wait("t", "V", vv)
            for k in range(4):
                for seg in range(2):
                    for ic in range(8):
                        c.inc("t", lambda eng, ic=ic, k=k, seg=seg, buf=buf,
                              st=(ic == 0), sp=(ic == 7):
                              eng.matmul(
                                  out=h.psB[:, (k % 2) * 1024 + seg * 512:
                                            (k % 2) * 1024 + (seg + 1) * 512],
                                  lhsT=h1s[:, ic * GS + k * 128: ic * GS + (k + 1) * 128],
                                  rhs=buf[:, 16384 + ic * 1024 + seg * 512:
                                          16384 + ic * 1024 + (seg + 1) * 512],
                                  start=st, stop=sp))
                tv = c.cnt["T"]
                c.wait("v", "T", tv)
                for seg in range(2):
                    if e < NE:
                        c.inc("v", lambda eng, k=k, seg=seg, e=e: eng.scalar_tensor_tensor(
                            out=y_acc[:, k * 1024 + seg * 512: k * 1024 + (seg + 1) * 512],
                            in0=h.psB[:, (k % 2) * 1024 + seg * 512:
                                      (k % 2) * 1024 + (seg + 1) * 512],
                            scalar=comb[:, k * 32 + e: k * 32 + e + 1],
                            in1=y_acc[:, k * 1024 + seg * 512: k * 1024 + (seg + 1) * 512],
                            op0=ALU.mult, op1=ALU.add))
                    else:
                        c.inc("v", lambda eng, k=k, seg=seg: eng.tensor_add(
                            out=y_acc[:, k * 1024 + seg * 512: k * 1024 + (seg + 1) * 512],
                            in0=y_acc[:, k * 1024 + seg * 512: k * 1024 + (seg + 1) * 512],
                            in1=h.psB[:, (k % 2) * 1024 + seg * 512:
                                      (k % 2) * 1024 + (seg + 1) * 512]))
                vv = c.cnt["V"]
                c.wait("t", "V", vv)
            wguardT[e % 2] = c.cnt["T"]
            if e + 1 <= NE:
                wready[(e + 1) % 2] = load_expert(e + 1, wguardT[(e + 1) % 2])
        if g < NG - 1:
            wready[0] = load_expert(0, wguardT[0])

        for k in range(4):
            c.wait("t", "V", c.cnt["V"])
            for seg in range(2):
                c.inc("t", lambda eng, k=k, seg=seg: eng.matmul(
                    out=h.psB[:, seg * 512:(seg + 1) * 512],
                    lhsT=combT[0:8, k * 128:(k + 1) * 128],
                    rhs=b2sb[:, seg * 512:(seg + 1) * 512], start=True, stop=True))
            tv = c.cnt["T"]
            lv = c.inc("y", lambda eng, s=x1_t[g * 4 + k]: eng.dma_start(out=xtile[:], in_=s))
            c.wait("v", "T", tv)
            c.inc("v", lambda eng, k=k: eng.tensor_add(out=y_acc[:, k * 1024:(k + 1) * 1024],
                                                       in0=y_acc[:, k * 1024:(k + 1) * 1024],
                                                       in1=h.psB[:, 0:1024]))
            c.inc("v", lambda eng, k=k: eng.tensor_add(out=y_acc[:, k * 1024:(k + 1) * 1024],
                                                       in0=y_acc[:, k * 1024:(k + 1) * 1024],
                                                       in1=h.sb2_b[:]))
            c.wait("v", "ld", lv)
            vv = c.inc("v", lambda eng, k=k: eng.tensor_add(
                out=y_acc[:, k * 1024:(k + 1) * 1024],
                in0=y_acc[:, k * 1024:(k + 1) * 1024], in1=xtile[:]))
            c.wait("t", "V", vv)
            c.wait("g", "V", vv)
            stv = c.inc("g", lambda eng, d=x1_t[g * 4 + k], s=y_acc[:, k * 1024:(k + 1) * 1024]:
                        eng.dma_start(out=d, in_=s), sname="st")
            c.wait("y", "st", stv)
            c.wait("v", "st", stv)


def emit_ln_full(c, h, es, io, scr, src_name, xn_fm, pitch, do_xprev=False, xprev_mask=None):
    nc = c.nc
    xtile = es.enter_context(_sbt(nc, "l_xtile", [128, 1024], F32))
    off = pitch - 2048
    src_t = io[src_name].rearrange("(n p) d -> n p d", p=128)
    for ti in range(NTILE):
        c.wait("y", "V", c.cnt["V"])
        c.wait("y", "S", c.cnt["S"])
        lv = c.inc("y", lambda eng, s=src_t[ti]: eng.dma_start(out=xtile[:], in_=s))
        c.wait("v", "ld", lv)
        c.wait("s", "ld", lv)
        emit_ln_rows(c, h, xtile[:], 128, scr,
                     lambda j, ti=ti: xn_fm[:, j * pitch + off + ti * 128:
                                            j * pitch + off + (ti + 1) * 128])
    if do_xprev:
        xp = es.enter_context(_sbt(nc, "l_xp", [1, 1024], F32))
        lv = c.inc("y", lambda eng: eng.dma_start(out=xp[:], in_=io["xprev"]))
        c.wait("v", "ld", lv)
        c.wait("s", "ld", lv)
        if xprev_mask is not None:
            vv = c.inc("v", lambda eng: eng.tensor_scalar(
                out=xp[:], in0=xp[:], scalar1=xprev_mask[0:1, 0:1], scalar2=None,
                op0=ALU.mult))
            c.wait("s", "V", vv)
        emit_ln_rows(c, h, xp[:], 1, scr,
                     lambda j: xn_fm[:, j * pitch: j * pitch + 1])


def emit_proj_fm(c, h, es, io, W_sb, src_fm, pitch, dst_fm, scale=None, act=None):
    for kc in range(8):
        for g in range(NG):
            seg = g % 4
            c.wait("t", "V", c.cnt["V"]) if seg == 0 else None
            for dc in range(8):
                c.inc("t", lambda eng,
                      l=W_sb[:, dc * 1024 + kc * 128: dc * 1024 + kc * 128 + 128],
                      r=src_fm[:, dc * pitch + (pitch - 2048) + g * 512:
                               dc * pitch + (pitch - 2048) + (g + 1) * 512],
                      o=h.psB[:, seg * 512:(seg + 1) * 512],
                      st=(dc == 0), sp=(dc == 7):
                      eng.matmul(out=o, lhsT=l, rhs=r, start=st, stop=sp))
            tv = c.cnt["T"]
            dst = dst_fm[:, kc * 2048 + g * 512: kc * 2048 + (g + 1) * 512]
            if act is not None:
                c.wait("s", "T", tv)
                sv = c.inc("s", lambda eng, dst=dst, seg=seg, act=act: eng.activation(
                    out=dst, in_=h.psB[:, seg * 512:(seg + 1) * 512], func=act))
                c.wait("t", "S", sv)
            elif scale is not None:
                c.wait("v", "T", tv)
                vv = c.inc("v", lambda eng, dst=dst, seg=seg, scale=scale: eng.tensor_scalar(
                    out=dst, in0=h.psB[:, seg * 512:(seg + 1) * 512], scalar1=scale,
                    scalar2=None, op0=ALU.mult))
                c.wait("t", "V", vv)
            else:
                c.wait("v", "T", tv)
                vv = c.inc("v", lambda eng, dst=dst, seg=seg: eng.tensor_copy(
                    out=dst, in_=h.psB[:, seg * 512:(seg + 1) * 512]))
                c.wait("t", "V", vv)


def emit_attn(c, h, es, io, Q_fm, attn_fm):
    nc = c.nc
    Kp = [es.enter_context(_sbt(nc, f"a_Kp{b}", [128, 4096], BF16)) for b in range(2)]
    K_tm = es.enter_context(_sbt(nc, "a_Ktm", [128, 2 * 32 * 65], BF16))
    expS = [es.enter_context(_sbt(nc, f"a_eS{b}", [128, 1024], BF16)) for b in range(2)]
    recb = es.enter_context(_sbt(nc, "a_recb", [1, 1024], F32))
    recbb = es.enter_context(_sbt(nc, "a_recbb", [64, 1024], F32))
    aodd = es.enter_context(_sbt(nc, "a_aodd", [64, 1024], BF16))

    vv = c.inc("v", lambda eng: eng.memset(K_tm[:], 1.0))
    c.wait("t", "V", vv)
    kp_src = io["K_pair"]
    kld = [0, 0]

    def load_kp(j):
        buf = Kp[j % 2]
        c.inc("y", lambda eng, d=buf[:, 0:2048], s=kp_src[j * 128:(j + 1) * 128, :]:
              eng.dma_start(out=d, in_=s))
        return c.inc("y", lambda eng, d=buf[:, 2048:4096],
                     s=kp_src[1024 + j * 128: 1024 + (j + 1) * 128, :]:
                     eng.dma_start(out=d, in_=s))

    kld[0] = load_kp(0)
    klast = [0, 0]
    for j in range(8):
        buf = Kp[j % 2]
        c.wait("t", "ld", kld[j % 2])
        for rnd in range(4):
            c.wait("t", "V", c.cnt["V"])
            for jj in range(8):
                cc = rnd * 8 + jj
                c.inc("t", lambda eng, i=buf[:, cc * 128:(cc + 1) * 128],
                      o=h.psTb[:, jj * 128:(jj + 1) * 128]:
                      eng.transpose(out=o, in_=i, identity=h.idbf[:]))
            tv = c.cnt["T"]
            c.wait("v", "T", tv)
            ps3 = h.psTb[:, 0:1024].rearrange("p (cc q) -> p cc q", q=128)
            for par in range(2):
                base = par * 2080 + rnd * 8 * 65
                dst = K_tm[:, base: base + 8 * 65].rearrange("p (cc q) -> p cc q", q=65)[:, :, 0:64]
                c.inc("v", lambda eng, dst=dst, src=ps3[:, :, par * 64:(par + 1) * 64]:
                      eng.tensor_copy(out=dst, in_=src))
            vv = c.cnt["V"]
            c.wait("t", "V", vv)
        for par in range(2):
            hq = Q_fm[par * 64:(par + 1) * 64, :]
            for qh in range(2):
                qs = j * 2048 + qh * 1024
                expv = [0, 0]
                for cc in range(32):
                    half = cc % 2
                    c.wait("t", "S", expv[half])
                    for q0 in (0, 512):
                        c.inc("t", lambda eng, l=buf[par * 64:(par + 1) * 64, cc * 128:(cc + 1) * 128],
                              r=hq[:, qs + q0: qs + q0 + 512],
                              o=h.psB[:, half * 1024 + q0: half * 1024 + q0 + 512]:
                              eng.matmul(out=o, lhsT=l, rhs=r, start=True, stop=True))
                    tv = c.cnt["T"]
                    c.wait("s", "T", tv)
                    expv[half] = c.inc("s", lambda eng, half=half: eng.activation(
                        out=expS[half][:], in_=h.psB[:, half * 1024:(half + 1) * 1024],
                        func=AF.Exp, bias=h.kv[:, 2:3]))
                    c.wait("t", "S", expv[half])
                    for q0 in (0, 512):
                        c.inc("t", lambda eng,
                              l=K_tm[:, par * 2080 + cc * 65: par * 2080 + (cc + 1) * 65],
                              r=expS[half][:, q0:q0 + 512],
                              o=h.psA[0:65, q0:q0 + 512], st=(cc == 0), sp=(cc == 31):
                              eng.matmul(out=o, lhsT=l, rhs=r, start=st, stop=sp))
                tv = c.cnt["T"]
                c.wait("v", "T", tv)
                vv = c.inc("v", lambda eng: eng.reciprocal(out=recb[:], in_=h.psA[64:65, 0:1024]))
                c.wait("t", "V", vv)
                for q0 in (0, 512):
                    c.inc("t", lambda eng, q0=q0: eng.matmul(
                        out=h.psB[0:64, q0:q0 + 512], lhsT=h.ones_row[0:1, 0:64],
                        rhs=recb[0:1, q0:q0 + 512], start=True, stop=True))
                tv = c.cnt["T"]
                c.wait("v", "T", tv)
                vv = c.inc("v", lambda eng: eng.tensor_copy(out=recbb[:], in_=h.psB[0:64, 0:1024]))
                c.wait("t", "V", vv)
                if par == 0:
                    dst = attn_fm[0:64, qs:qs + 1024]
                    vv = c.inc("v", lambda eng, dst=dst: eng.tensor_mul(
                        out=dst, in0=h.psA[0:64, 0:1024], in1=recbb[:]))
                    c.wait("t", "V", vv)
                else:
                    vv = c.inc("v", lambda eng: eng.tensor_mul(
                        out=aodd[:], in0=h.psA[0:64, 0:1024], in1=recbb[:]))
                    c.wait("g", "V", vv)
                    stv = c.inc("g", lambda eng, d=attn_fm[64:128, qs:qs + 1024]:
                                eng.dma_start(out=d, in_=aodd[:]), sname="st")
                    c.wait("v", "st", stv)
                    c.wait("t", "V", vv)
        klast[j % 2] = c.cnt["T"]
        if j < 7:
            c.wait("y", "T", klast[(j + 1) % 2])
            kld[(j + 1) % 2] = load_kp(j + 1)


def build_A(dbg=False):
    nc = bass.Bass("TRN2", target_bir_lowering=False, debug=False)
    io = {}
    io["u"] = nc.dram_tensor("u", [T, D], F32, kind="ExternalInput").ap()
    io["uh"] = nc.dram_tensor("uh", [3, D], F32, kind="ExternalInput").ap()
    io["W_inT"] = nc.dram_tensor("W_inT", [D, RIN], F32, kind="ExternalInput").ap()
    for nm, shp in (("conv_w_fm", [128, 72]), ("conv_b_fm", [128, 18]),
                    ("A_log_r", [1, 32]), ("dt_bias_r", [1, 32]),
                    ("ident32", [128, 128]), ("U128", [128, 128]), ("TRIU32", [128, 128]),
                    ("SLT32", [128, 128]),
                    ("ones_row", [1, 128]), ("ones_col", [128, 1])):
        io[nm] = nc.dram_tensor(nm, shp, F32, kind="ExternalInput").ap()
    for nm, shp in (("identbf", [128, 128]), ("SLT", [128, 128])):
        io[nm] = nc.dram_tensor(nm, shp, BF16, kind="ExternalInput").ap()
    io["S_out"] = nc.dram_tensor("S_out", [128, DI], F32, kind="ExternalOutput").ap()
    if dbg:
        for nm, shp, dt in (("dbg_tails", [128, 54], F32),
                            ("dbg_xg", [128, 8 * 256], BF16), ("dbg_xfm", [128, 18 * 256], BF16),
                            ("dbg_dtT", [128, 32], F32), ("dbg_aT", [128, 32], F32),
                            ("dbg_AcumT", [128, 32], F32), ("dbg_wT", [128, 32], F32),
                            ("dbg_eTb", [128, 32], F32), ("dbg_xdw", [128, DI], BF16),
                            ("dbg_Btm", [128, 128], BF16), ("dbg_S", [128, DI], F32)):
            io[nm] = nc.dram_tensor(nm, shp, dt, kind="ExternalOutput").ap()

    es = contextlib.ExitStack()
    c = Prog(nc, es)
    with nc.Block() as blk:
        h = emit_prelude(c, io, es, need=("mamba",))
        scr = _mk_scr(c, es)
        emit_mamba(c, h, es, io, scr, full=False, dbg=dbg)
        c.barrier()
        c.build(blk)
    es.close()
    return nc


def build_B(dbg=False, dbg_tile=(0, 0)):
    nc = bass.Bass("TRN2", target_bir_lowering=False, debug=False)
    io = {}
    io["u"] = nc.dram_tensor("u", [T, D], F32, kind="ExternalInput").ap()
    io["uh"] = nc.dram_tensor("uh", [3, D], F32, kind="ExternalInput").ap()
    io["S_in"] = nc.dram_tensor("S_in", [128, DI], F32, kind="ExternalInput").ap()
    f32_ins = [("W_inT", [D, RIN]), ("W_outT", [DI, D]),
               ("conv_w_fm", [128, 72]), ("conv_b_fm", [128, 18]),
               ("A_log_r", [1, 32]), ("dt_bias_r", [1, 32]), ("D_r", [1, 32]),
               ("norm_w_r", [1, DI]),
               ("gate_wT", [D, 8]), ("moe_w1", [NE, D, INTER]), ("moe_w2", [NE, INTER, D]),
               ("moe_w3", [NE, D, INTER]), ("moe_b1_fm", [NE, 128, 8]),
               ("moe_b3_fm", [NE, 128, 8]), ("moe_b2", [NE, INTER]),
               ("sw1", [D, INTER]), ("sw2", [INTER, D]), ("sw3", [D, INTER]),
               ("sb1_fm", [128, 8]), ("sb3_fm", [128, 8]), ("sb2_r", [1, INTER]),
               ("Wk1", [D, D]),
               ("ident32", [128, 128]), ("U128", [128, 128]), ("TRIU32", [128, 128]),
               ("SLT32", [128, 128]),
               ("ones_row", [1, 128]), ("ones_col", [128, 1])]
    for nm, shp in f32_ins:
        io[nm] = nc.dram_tensor(nm, shp, F32, kind="ExternalInput").ap()
    for nm, shp in (("identbf", [128, 128]), ("SLT", [128, 128])):
        io[nm] = nc.dram_tensor(nm, shp, BF16, kind="ExternalInput").ap()
    io["x1"] = nc.dram_tensor("x1", [T, D], F32, kind="ExternalOutput").ap()
    io["K_own"] = nc.dram_tensor("K_own", [D, T], BF16, kind="ExternalOutput").ap()
    if dbg:
        for nm, shp, dt in (("dbg_tails", [128, 54], F32),
                            ("dbg_xg", [128, 8 * 256], BF16), ("dbg_xfm", [128, 18 * 256], BF16),
                            ("dbg_dtT", [128, 32], F32), ("dbg_aT", [128, 32], F32),
                            ("dbg_AcumT", [128, 32], F32), ("dbg_wT", [128, 32], F32),
                            ("dbg_eTb", [128, 32], F32), ("dbg_xdw", [128, DI], BF16),
                            ("dbg_Btm", [128, 128], BF16), ("dbg_S", [128, DI], F32),
                            ("dbg_xd", [128, DI], BF16), ("dbg_uT", [128, 32], F32),
                            ("dbg_siluz", [128, DI], BF16), ("dbg_t1", [128, DI], F32),
                            ("dbg_yn", [128, DI], BF16), ("dbg_CBm", [128, 128], BF16),
                            ("dbg_pre", [128, DI], F32), ("dbg_gate", [128, DI], F32),
                            ("dbg_Db", [128, 32], F32), ("dbg_normwb", [128, DI], F32),
                            ("dbg_rms", [128, 4], F32), ("dbg_Mtq", [128, 1024], BF16),
                            ("dbg_amq", [128, 1024], F32)):
            io[nm] = nc.dram_tensor(nm, shp, dt, kind="ExternalOutput").ap()

    es = contextlib.ExitStack()
    c = Prog(nc, es)
    with nc.Block() as blk:
        h = emit_prelude(c, io, es, need=("mamba", "mamba_full", "moe"))
        scr = _mk_scr(c, es)
        with contextlib.ExitStack() as es_m:
            emit_mamba(c, h, es_m, io, scr, full=True, dbg=dbg, dbg_tile=dbg_tile)
            c.barrier()
        if upto >= 4:
            with contextlib.ExitStack() as es_e:
                emit_moe(c, h, es_e, io, scr)
                c.barrier()
        with contextlib.ExitStack() as es_k:
            xn1 = es_k.enter_context(_sbt(nc, "b_xn1", [128, 8 * 2049], BF16))
            emit_ln_full(c, h, es_k, io, scr, "x1", xn1, 2049, do_xprev=False)
            c.barrier()
            Wk = emit_load_wchunks(c, es_k, io, "Wk1", D, D, tag="wk1")
            Ko = es_k.enter_context(_sbt(nc, "b_Ko", [128, 8 * 2048], BF16))
            emit_proj_fm(c, h, es_k, io, Wk, xn1[:], 2049, Ko[:])
            c.barrier()
            for kc in range(8):
                c.inc("g", lambda eng, d=io["K_own"][kc * 128:(kc + 1) * 128, :],
                      s=Ko[:, kc * 2048:(kc + 1) * 2048]: eng.dma_start(out=d, in_=s),
                      sname="st")
            c.wait("g", "st", c.cnt["st"])
        c.barrier()
        c.build(blk)
    es.close()
    return nc


def build_C():
    nc = bass.Bass("TRN2", target_bir_lowering=False, debug=False)
    io = {}
    io["x1"] = nc.dram_tensor("x1", [T, D], F32, kind="ExternalInput").ap()
    io["xprev"] = nc.dram_tensor("xprev", [1, D], F32, kind="ExternalInput").ap()
    io["K_pair"] = nc.dram_tensor("K_pair", [2048, T], BF16, kind="ExternalInput").ap()
    for nm, shp in (("Wq1", [D, D]), ("Wo1", [D, D]), ("bo_r", [1, D]),
                    ("WrT", [D, D]), ("WkT", [D, D]), ("WvT", [D, D]),
                    ("mu_r_fm", [128, 8]), ("mu_k_fm", [128, 8]),
                    ("ident32", [128, 128]), ("ones_row", [1, 128]), ("ones_col", [128, 1])):
        io[nm] = nc.dram_tensor(nm, shp, F32, kind="ExternalInput").ap()
    io["identbf"] = nc.dram_tensor("identbf", [128, 128], BF16, kind="ExternalInput").ap()
    io["out"] = nc.dram_tensor("out", [T, D], F32, kind="ExternalOutput").ap()

    es = contextlib.ExitStack()
    c = Prog(nc, es)
    with nc.Block() as blk:
        h = emit_prelude(c, io, es, need=("l1",))
        scr = _mk_scr(c, es)
        xn1 = es.enter_context(_sbt(nc, "c_xn1", [128, 8 * 2049], BF16))
        emit_ln_full(c, h, es, io, scr, "x1", xn1, 2049, do_xprev=True)
        c.barrier()
        x1_t = io["x1"].rearrange("(n p) d -> n p d", p=128)
        out_t = io["out"].rearrange("(n p) d -> n p d", p=128)
        with contextlib.ExitStack() as es_a:
            Wq = emit_load_wchunks(c, es_a, io, "Wq1", D, D, tag="wq1")
            Q_fm = es_a.enter_context(_sbt(nc, "c_Q", [128, 8 * 2048], BF16))
            emit_proj_fm(c, h, es_a, io, Wq, xn1[:], 2049, Q_fm[:], scale=1.0 / 32.0)
            c.barrier()
            attn_fm = es_a.enter_context(_sbt(nc, "c_attn", [128, 8 * 2048], BF16))
            emit_attn(c, h, es_a, io, Q_fm[:], attn_fm[:])
            c.barrier()
            Wo = emit_load_wchunks(c, es_a, io, "Wo1", D, D, tag="wo1")
            xtile = es_a.enter_context(_sbt(nc, "c_xt", [128, 1024], F32))
            xo = es_a.enter_context(_sbt(nc, "c_xo", [128, 1024], F32))
            for ti in range(NTILE):
                c.wait("y", "st", c.cnt["st"])
                lv = c.inc("y", lambda eng, s=x1_t[ti]: eng.dma_start(out=xtile[:], in_=s))
                c.wait("t", "V", c.cnt["V"])
                for seg in range(2):
                    for ec in range(8):
                        c.inc("t", lambda eng, ec=ec, seg=seg, ti=ti, st=(ec == 0), sp=(ec == 7):
                              eng.matmul(
                                  out=h.psB[:, seg * 512:(seg + 1) * 512],
                                  lhsT=attn_fm[:, ec * 2048 + ti * 128: ec * 2048 + (ti + 1) * 128],
                                  rhs=Wo[:, ec * 1024 + seg * 512: ec * 1024 + (seg + 1) * 512],
                                  start=st, stop=sp))
                tv = c.cnt["T"]
                c.wait("v", "T", tv)
                c.wait("v", "ld", lv)
                c.inc("v", lambda eng: eng.tensor_add(out=xo[:], in0=h.psB[:, 0:1024],
                                                      in1=h.bo_b[:]))
                vv = c.inc("v", lambda eng: eng.tensor_add(out=xo[:], in0=xo[:], in1=xtile[:]))
                c.wait("t", "V", vv)
                c.wait("g", "V", vv)
                stv = c.inc("g", lambda eng, d=out_t[ti]: eng.dma_start(out=d, in_=xo[:]),
                            sname="st")
                c.wait("v", "st", stv)
        c.barrier()
        with contextlib.ExitStack() as es_f:
            r_all = es_f.enter_context(_sbt(nc, "c_rall", [128, 16 * 1024], BF16))
            with contextlib.ExitStack() as es_r:
                WrT = emit_load_wchunks(c, es_r, io, "WrT", D, D, tag="wrt")
                xr = es_r.enter_context(_sbt(nc, "c_xr", [128, 8 * 2048], BF16))
                for dc in range(8):
                    c.inc("v", lambda eng, dc=dc: eng.tensor_scalar(
                        out=xr[:, dc * 2048:(dc + 1) * 2048],
                        in0=xn1[:, dc * 2049: dc * 2049 + 2048],
                        scalar1=h.omur[:, dc:dc + 1], scalar2=None, op0=ALU.mult))
                    vv = c.inc("v", lambda eng, dc=dc: eng.scalar_tensor_tensor(
                        out=xr[:, dc * 2048:(dc + 1) * 2048],
                        in0=xn1[:, dc * 2049 + 1: dc * 2049 + 2049],
                        scalar=h.mur[:, dc:dc + 1],
                        in1=xr[:, dc * 2048:(dc + 1) * 2048], op0=ALU.mult, op1=ALU.add))
                c.wait("t", "V", vv)
                for ti in range(NTILE):
                    for seg in range(2):
                        for dc in range(8):
                            c.inc("t", lambda eng, dc=dc, seg=seg, ti=ti, st=(dc == 0), sp=(dc == 7):
                                  eng.matmul(
                                      out=h.psB[:, seg * 512:(seg + 1) * 512],
                                      lhsT=xr[:, dc * 2048 + ti * 128: dc * 2048 + (ti + 1) * 128],
                                      rhs=WrT[:, dc * 1024 + seg * 512: dc * 1024 + (seg + 1) * 512],
                                      start=st, stop=sp))
                    tv = c.cnt["T"]
                    c.wait("s", "T", tv)
                    sv = c.inc("s", lambda eng, ti=ti: eng.activation(
                        out=r_all[:, ti * 1024:(ti + 1) * 1024], in_=h.psB[:, 0:1024],
                        func=AF.Sigmoid))
                    c.wait("t", "S", sv)
            c.barrier()
            kact = es_f.enter_context(_sbt(nc, "c_kact", [128, 8 * 2048], BF16))
            with contextlib.ExitStack() as es_k2:
                WkT = emit_load_wchunks(c, es_k2, io, "WkT", D, D, tag="wkt")
                xk = es_k2.enter_context(_sbt(nc, "c_xk", [128, 8 * 2048], BF16))
                scrk = es_k2.enter_context(_sbt(nc, "c_scrk", [128, 512], F32))
                for dc in range(8):
                    c.inc("v", lambda eng, dc=dc: eng.tensor_scalar(
                        out=xk[:, dc * 2048:(dc + 1) * 2048],
                        in0=xn1[:, dc * 2049: dc * 2049 + 2048],
                        scalar1=h.omuk[:, dc:dc + 1], scalar2=None, op0=ALU.mult))
                    vv = c.inc("v", lambda eng, dc=dc: eng.scalar_tensor_tensor(
                        out=xk[:, dc * 2048:(dc + 1) * 2048],
                        in0=xn1[:, dc * 2049 + 1: dc * 2049 + 2049],
                        scalar=h.muk[:, dc:dc + 1],
                        in1=xk[:, dc * 2048:(dc + 1) * 2048], op0=ALU.mult, op1=ALU.add))
                c.wait("t", "V", vv)
                for kc in range(8):
                    for g in range(NG):
                        seg = g % 4
                        for dc in range(8):
                            c.inc("t", lambda eng, dc=dc, kc=kc, g=g, seg=seg,
                                  st=(dc == 0), sp=(dc == 7):
                                  eng.matmul(
                                      out=h.psB[:, seg * 512:(seg + 1) * 512],
                                      lhsT=WkT[:, dc * 1024 + kc * 128: dc * 1024 + kc * 128 + 128],
                                      rhs=xk[:, dc * 2048 + g * 512: dc * 2048 + (g + 1) * 512],
                                      start=st, stop=sp))
                        tv = c.cnt["T"]
                        c.wait("s", "T", tv)
                        c.inc("s", lambda eng, seg=seg: eng.activation(
                            out=scrk[:], in_=h.psB[:, seg * 512:(seg + 1) * 512], func=AF.Relu))
                        sv = c.inc("s", lambda eng, kc=kc, g=g: eng.activation(
                            out=kact[:, kc * 2048 + g * 512: kc * 2048 + (g + 1) * 512],
                            in_=scrk[:], func=AF.Square))
                        c.wait("t", "S", sv)
            c.barrier()
            with contextlib.ExitStack() as es_v:
                WvT = emit_load_wchunks(c, es_v, io, "WvT", D, D, tag="wvt")
                otile = es_v.enter_context(_sbt(nc, "c_ot", [128, 1024], F32))
                fout = es_v.enter_context(_sbt(nc, "c_fout", [128, 1024], F32))
                for ti in range(NTILE):
                    c.wait("y", "st", c.cnt["st"])
                    lv = c.inc("y", lambda eng, s=out_t[ti]: eng.dma_start(out=otile[:], in_=s))
                    c.wait("t", "V", c.cnt["V"])
                    for seg in range(2):
                        for ic in range(8):
                            c.inc("t", lambda eng, ic=ic, seg=seg, ti=ti, st=(ic == 0), sp=(ic == 7):
                                  eng.matmul(
                                      out=h.psB[:, seg * 512:(seg + 1) * 512],
                                      lhsT=kact[:, ic * 2048 + ti * 128: ic * 2048 + (ti + 1) * 128],
                                      rhs=WvT[:, ic * 1024 + seg * 512: ic * 1024 + (seg + 1) * 512],
                                      start=st, stop=sp))
                    tv = c.cnt["T"]
                    c.wait("v", "T", tv)
                    c.wait("v", "ld", lv)
                    c.inc("v", lambda eng, ti=ti: eng.tensor_mul(
                        out=fout[:], in0=h.psB[:, 0:1024],
                        in1=r_all[:, ti * 1024:(ti + 1) * 1024]))
                    vv = c.inc("v", lambda eng: eng.tensor_add(out=fout[:], in0=fout[:],
                                                               in1=otile[:]))
                    c.wait("t", "V", vv)
                    c.wait("g", "V", vv)
                    stv = c.inc("g", lambda eng, d=out_t[ti]: eng.dma_start(out=d, in_=fout[:]),
                                sname="st")
                    c.wait("v", "st", stv)
        c.barrier()
        c.wait("g", "st", c.cnt["st"])
        c.build(blk)
    es.close()
    return nc




def build_fused(no_cc=False, cc_set="SPK", upto=8, needs=("mamba", "mamba_full", "moe", "l1")):
    nc = bass.Bass("TRN2", target_bir_lowering=False, debug=False)
    io = {}
    io["u"] = nc.dram_tensor("u", [T, D], F32, kind="ExternalInput").ap()
    io["uh"] = nc.dram_tensor("uh", [3, D], F32, kind="ExternalInput").ap()
    f32_ins = [("W_inT", [D, RIN]), ("W_outT", [DI, D]),
               ("conv_w_fm", [128, 72]), ("conv_b_fm", [128, 18]),
               ("A_log_r", [1, 32]), ("dt_bias_r", [1, 32]), ("D_r", [1, 32]),
               ("norm_w_r", [1, DI]),
               ("gate_wT", [D, 8]), ("moe_w1", [NE, D, INTER]), ("moe_w2", [NE, INTER, D]),
               ("moe_w3", [NE, D, INTER]), ("moe_b1_fm", [NE, 128, 8]),
               ("moe_b3_fm", [NE, 128, 8]), ("moe_b2", [NE, INTER]),
               ("sw1", [D, INTER]), ("sw2", [INTER, D]), ("sw3", [D, INTER]),
               ("sb1_fm", [128, 8]), ("sb3_fm", [128, 8]), ("sb2_r", [1, INTER]),
               ("Wk1", [D, D]), ("Wq1", [D, D]), ("Wo1", [D, D]), ("bo_r", [1, D]),
               ("WrT", [D, D]), ("WkT", [D, D]), ("WvT", [D, D]),
               ("mu_r_fm", [128, 8]), ("mu_k_fm", [128, 8]),
               ("smask", [128, 1]),
               ("ident32", [128, 128]), ("U128", [128, 128]), ("TRIU32", [128, 128]),
               ("SLT32", [128, 128]),
               ("ones_row", [1, 128]), ("ones_col", [128, 1])]
    for nm, shp in f32_ins:
        io[nm] = nc.dram_tensor(nm, shp, F32, kind="ExternalInput").ap()
    for nm, shp in (("identbf", [128, 128]), ("SLT", [128, 128])):
        io[nm] = nc.dram_tensor(nm, shp, BF16, kind="ExternalInput").ap()
    io["out"] = nc.dram_tensor("out", [T, D], F32, kind="ExternalOutput").ap()
    io["x1"] = nc.dram_tensor("x1d", [T, D], F32).ap()
    ccS_in = nc.dram_tensor("ccS_in", [128, DI], F32)
    ccS_out = nc.dram_tensor("ccS_out", [256, DI], F32)
    ccK_in = nc.dram_tensor("ccK_in", [D, T], BF16)
    ccK_out = nc.dram_tensor("ccK_out", [2 * D, T], BF16)
    ccP_in = nc.dram_tensor("ccP_in", [1, D], F32)
    ccP_out = nc.dram_tensor("ccP_out", [2, D], F32)
    io["S_in"] = ccS_out.ap()[0:128, :]
    io["K_pair"] = ccK_out.ap()
    io["xprev"] = ccP_out.ap()[0:1, :]
    RG = [[0, 1], [2, 3], [4, 5], [6, 7]]

    es = contextlib.ExitStack()
    c = Prog(nc, es)
    with nc.Block() as blk:
        h = emit_prelude(c, io, es, need=needs)
        io["smask_sb"] = es.enter_context(_sbt(nc, "f_smask", [128, 1], F32))
        lv = c.inc("y", lambda eng: eng.dma_start(out=io["smask_sb"][:], in_=io["smask"]))
        for e in ("v", "s"):
            c.wait(e, "ld", lv)
        scr = _mk_scr(c, es)
        if upto >= 2:
            with contextlib.ExitStack() as es_a:
                emit_mamba(c, h, es_a, io, scr, full=False, state_dst=ccS_in.ap())
                c.barrier()
        if no_cc or "S" not in cc_set:
            c.inc("g", lambda eng: eng.dma_start(out=ccS_out.ap()[0:128, :],
                                                 in_=ccS_in.ap()), sname="st")
        else:
            c.inc_direct("g", lambda eng: eng.collective_compute(
                "AllGather", ALU.bypass, replica_groups=RG,
                ins=[ccS_in.ap().opt()], outs=[ccS_out.ap().opt()]), sname="G")
        c.barrier()
        if upto >= 3:
            with contextlib.ExitStack() as es_m:
                emit_mamba(c, h, es_m, io, scr, full=True)
                c.barrier()
        if upto >= 4:
            with contextlib.ExitStack() as es_e:
                emit_moe(c, h, es_e, io, scr)
                c.barrier()
        if upto < 5:
            out_t0 = io["out"].rearrange("(n p) d -> n p d", p=128)
            u_t0 = io["u"].rearrange("(n p) d -> n p d", p=128)
            for ti in range(NTILE):
                c.inc("g", lambda eng, d=out_t0[ti], s=u_t0[ti]: eng.dma_start(out=d, in_=s),
                      sname="st")
            c.wait("g", "st", c.cnt["st"])
            c.build(blk)
            es.close()
            return nc
        c.inc("g", lambda eng: eng.dma_start(out=ccP_in.ap(), in_=io["x1"][T - 1:T, :]),
              sname="st")
        c.wait("g", "st", c.cnt["st"])
        if no_cc or "P" not in cc_set:
            c.inc("g", lambda eng: eng.dma_start(out=ccP_out.ap()[0:1, :],
                                                 in_=ccP_in.ap()), sname="st")
        else:
            c.inc_direct("g", lambda eng: eng.collective_compute(
                "AllGather", ALU.bypass, replica_groups=RG,
                ins=[ccP_in.ap().opt()], outs=[ccP_out.ap().opt()]), sname="G")
        c.barrier()
        if upto < 6:
            out_t0 = io["out"].rearrange("(n p) d -> n p d", p=128)
            u_t0 = io["u"].rearrange("(n p) d -> n p d", p=128)
            for ti in range(NTILE):
                c.inc("g", lambda eng, d=out_t0[ti], s=u_t0[ti]: eng.dma_start(out=d, in_=s),
                      sname="st")
            c.wait("g", "st", c.cnt["st"])
            c.build(blk)
            es.close()
            return nc
        xn1 = es.enter_context(_sbt(nc, "f_xn1", [128, 8 * 2049], BF16))
        emit_ln_full(c, h, es, io, scr, "x1", xn1, 2049, do_xprev=True,
                     xprev_mask=io["smask_sb"])
        c.barrier()
        with contextlib.ExitStack() as es_k:
            Wk = emit_load_wchunks(c, es_k, io, "Wk1", D, D, tag="wk1")
            Ko = es_k.enter_context(_sbt(nc, "f_Ko", [128, 8 * 2048], BF16))
            emit_proj_fm(c, h, es_k, io, Wk, xn1[:], 2049, Ko[:])
            c.barrier()
            for kc in range(8):
                c.inc("g", lambda eng, d=ccK_in.ap()[kc * 128:(kc + 1) * 128, :],
                      s=Ko[:, kc * 2048:(kc + 1) * 2048]: eng.dma_start(out=d, in_=s),
                      sname="st")
            c.wait("g", "st", c.cnt["st"])
            if no_cc or "K" not in cc_set:
                c.inc("g", lambda eng: eng.dma_start(out=ccK_out.ap()[0:1024, :],
                                                     in_=ccK_in.ap()), sname="st")
            else:
                c.inc_direct("g", lambda eng: eng.collective_compute(
                    "AllGather", ALU.bypass, replica_groups=RG,
                    ins=[ccK_in.ap().opt()], outs=[ccK_out.ap().opt()]), sname="G")
        c.barrier()
        if upto < 7:
            out_t0 = io["out"].rearrange("(n p) d -> n p d", p=128)
            u_t0 = io["u"].rearrange("(n p) d -> n p d", p=128)
            for ti in range(NTILE):
                c.inc("g", lambda eng, d=out_t0[ti], s=u_t0[ti]: eng.dma_start(out=d, in_=s),
                      sname="st")
            c.wait("g", "st", c.cnt["st"])
            c.build(blk)
            es.close()
            return nc
        x1_t = io["x1"].rearrange("(n p) d -> n p d", p=128)
        out_t = io["out"].rearrange("(n p) d -> n p d", p=128)
        with contextlib.ExitStack() as es_q:
            Wq = emit_load_wchunks(c, es_q, io, "Wq1", D, D, tag="wq1")
            Q_fm = es_q.enter_context(_sbt(nc, "f_Q", [128, 8 * 2048], BF16))
            emit_proj_fm(c, h, es_q, io, Wq, xn1[:], 2049, Q_fm[:], scale=1.0 / 32.0)
            c.barrier()
            attn_fm = es_q.enter_context(_sbt(nc, "f_attn", [128, 8 * 2048], BF16))
            emit_attn(c, h, es_q, io, Q_fm[:], attn_fm[:])
            c.barrier()
            Wo = emit_load_wchunks(c, es_q, io, "Wo1", D, D, tag="wo1")
            xtile = es_q.enter_context(_sbt(nc, "f_xt", [128, 1024], F32))
            xo = es_q.enter_context(_sbt(nc, "f_xo", [128, 1024], F32))
            for ti in range(NTILE):
                c.wait("y", "st", c.cnt["st"])
                lv = c.inc("y", lambda eng, s=x1_t[ti]: eng.dma_start(out=xtile[:], in_=s))
                c.wait("t", "V", c.cnt["V"])
                for seg in range(2):
                    for ec in range(8):
                        c.inc("t", lambda eng, ec=ec, seg=seg, ti=ti,
                              st=(ec == 0), sp=(ec == 7):
                              eng.matmul(
                                  out=h.psB[:, seg * 512:(seg + 1) * 512],
                                  lhsT=attn_fm[:, ec * 2048 + ti * 128:
                                               ec * 2048 + (ti + 1) * 128],
                                  rhs=Wo[:, ec * 1024 + seg * 512:
                                         ec * 1024 + (seg + 1) * 512],
                                  start=st, stop=sp))
                tv = c.cnt["T"]
                c.wait("v", "T", tv)
                c.wait("v", "ld", lv)
                c.inc("v", lambda eng: eng.tensor_add(out=xo[:], in0=h.psB[:, 0:1024],
                                                      in1=h.bo_b[:]))
                vv = c.inc("v", lambda eng: eng.tensor_add(out=xo[:], in0=xo[:],
                                                           in1=xtile[:]))
                c.wait("t", "V", vv)
                c.wait("g", "V", vv)
                stv = c.inc("g", lambda eng, d=out_t[ti]: eng.dma_start(out=d, in_=xo[:]),
                            sname="st")
                c.wait("v", "st", stv)
        c.barrier()
        with contextlib.ExitStack() as es_f:
            r_all = es_f.enter_context(_sbt(nc, "f_rall", [128, 16 * 1024], BF16))
            with contextlib.ExitStack() as es_r:
                WrT = emit_load_wchunks(c, es_r, io, "WrT", D, D, tag="wrt")
                xr = es_r.enter_context(_sbt(nc, "f_xr", [128, 8 * 2048], BF16))
                for dc in range(8):
                    c.inc("v", lambda eng, dc=dc: eng.tensor_scalar(
                        out=xr[:, dc * 2048:(dc + 1) * 2048],
                        in0=xn1[:, dc * 2049: dc * 2049 + 2048],
                        scalar1=h.omur[:, dc:dc + 1], scalar2=None, op0=ALU.mult))
                    vv = c.inc("v", lambda eng, dc=dc: eng.scalar_tensor_tensor(
                        out=xr[:, dc * 2048:(dc + 1) * 2048],
                        in0=xn1[:, dc * 2049 + 1: dc * 2049 + 2049],
                        scalar=h.mur[:, dc:dc + 1],
                        in1=xr[:, dc * 2048:(dc + 1) * 2048], op0=ALU.mult, op1=ALU.add))
                c.wait("t", "V", vv)
                for ti in range(NTILE):
                    for seg in range(2):
                        for dc in range(8):
                            c.inc("t", lambda eng, dc=dc, seg=seg, ti=ti,
                                  st=(dc == 0), sp=(dc == 7):
                                  eng.matmul(
                                      out=h.psB[:, seg * 512:(seg + 1) * 512],
                                      lhsT=xr[:, dc * 2048 + ti * 128:
                                              dc * 2048 + (ti + 1) * 128],
                                      rhs=WrT[:, dc * 1024 + seg * 512:
                                              dc * 1024 + (seg + 1) * 512],
                                      start=st, stop=sp))
                    tv = c.cnt["T"]
                    c.wait("s", "T", tv)
                    sv = c.inc("s", lambda eng, ti=ti: eng.activation(
                        out=r_all[:, ti * 1024:(ti + 1) * 1024], in_=h.psB[:, 0:1024],
                        func=AF.Sigmoid))
                    c.wait("t", "S", sv)
            c.barrier()
            kact = es_f.enter_context(_sbt(nc, "f_kact", [128, 8 * 2048], BF16))
            with contextlib.ExitStack() as es_k2:
                WkT = emit_load_wchunks(c, es_k2, io, "WkT", D, D, tag="wkt")
                xk = es_k2.enter_context(_sbt(nc, "f_xk", [128, 8 * 2048], BF16))
                scrk = es_k2.enter_context(_sbt(nc, "f_scrk", [128, 512], F32))
                for dc in range(8):
                    c.inc("v", lambda eng, dc=dc: eng.tensor_scalar(
                        out=xk[:, dc * 2048:(dc + 1) * 2048],
                        in0=xn1[:, dc * 2049: dc * 2049 + 2048],
                        scalar1=h.omuk[:, dc:dc + 1], scalar2=None, op0=ALU.mult))
                    vv = c.inc("v", lambda eng, dc=dc: eng.scalar_tensor_tensor(
                        out=xk[:, dc * 2048:(dc + 1) * 2048],
                        in0=xn1[:, dc * 2049 + 1: dc * 2049 + 2049],
                        scalar=h.muk[:, dc:dc + 1],
                        in1=xk[:, dc * 2048:(dc + 1) * 2048], op0=ALU.mult, op1=ALU.add))
                c.wait("t", "V", vv)
                for kc in range(8):
                    for g in range(4):
                        seg = g % 4
                        for dc in range(8):
                            c.inc("t", lambda eng, dc=dc, kc=kc, g=g, seg=seg,
                                  st=(dc == 0), sp=(dc == 7):
                                  eng.matmul(
                                      out=h.psB[:, seg * 512:(seg + 1) * 512],
                                      lhsT=WkT[:, dc * 1024 + kc * 128:
                                               dc * 1024 + kc * 128 + 128],
                                      rhs=xk[:, dc * 2048 + g * 512:
                                             dc * 2048 + (g + 1) * 512],
                                      start=st, stop=sp))
                        tv = c.cnt["T"]
                        c.wait("s", "T", tv)
                        c.inc("s", lambda eng, seg=seg: eng.activation(
                            out=scrk[:], in_=h.psB[:, seg * 512:(seg + 1) * 512],
                            func=AF.Relu))
                        sv = c.inc("s", lambda eng, kc=kc, g=g: eng.activation(
                            out=kact[:, kc * 2048 + g * 512: kc * 2048 + (g + 1) * 512],
                            in_=scrk[:], func=AF.Square))
                        c.wait("t", "S", sv)
            c.barrier()
            with contextlib.ExitStack() as es_v:
                WvT = emit_load_wchunks(c, es_v, io, "WvT", D, D, tag="wvt")
                otile = es_v.enter_context(_sbt(nc, "f_ot", [128, 1024], F32))
                fout = es_v.enter_context(_sbt(nc, "f_fout", [128, 1024], F32))
                for ti in range(NTILE):
                    c.wait("y", "st", c.cnt["st"])
                    lv = c.inc("y", lambda eng, s=out_t[ti]: eng.dma_start(out=otile[:], in_=s))
                    c.wait("t", "V", c.cnt["V"])
                    for seg in range(2):
                        for ic in range(8):
                            c.inc("t", lambda eng, ic=ic, seg=seg, ti=ti,
                                  st=(ic == 0), sp=(ic == 7):
                                  eng.matmul(
                                      out=h.psB[:, seg * 512:(seg + 1) * 512],
                                      lhsT=kact[:, ic * 2048 + ti * 128:
                                                ic * 2048 + (ti + 1) * 128],
                                      rhs=WvT[:, ic * 1024 + seg * 512:
                                              ic * 1024 + (seg + 1) * 512],
                                      start=st, stop=sp))
                    tv = c.cnt["T"]
                    c.wait("v", "T", tv)
                    c.wait("v", "ld", lv)
                    c.inc("v", lambda eng, ti=ti: eng.tensor_mul(
                        out=fout[:], in0=h.psB[:, 0:1024],
                        in1=r_all[:, ti * 1024:(ti + 1) * 1024]))
                    vv = c.inc("v", lambda eng: eng.tensor_add(out=fout[:], in0=fout[:],
                                                               in1=otile[:]))
                    c.wait("t", "V", vv)
                    c.wait("g", "V", vv)
                    stv = c.inc("g", lambda eng, d=out_t[ti]: eng.dma_start(
                        out=d, in_=fout[:]), sname="st")
                    c.wait("v", "st", stv)
        c.barrier()
        c.wait("g", "st", c.cnt["st"])
        c.build(blk)
    es.close()
    return nc


_CACHE = {}


def _get(name, fn):
    if name not in _CACHE:
        _CACHE[name] = fn()
    return _CACHE[name]


FUSED = True


def kernel(**inputs):
    if FUSED:
        return kernel_fused(**inputs)
    return kernel_unfused(**inputs)


def kernel_fused(**inputs):
    w = prep_weights(inputs)
    u = np.ascontiguousarray(np.asarray(inputs["L0_attn_u"], dtype=np.float32))
    cores = list(range(8))
    nc = _get("F", build_fused)
    wkeys = ("W_inT", "W_outT", "conv_w_fm", "conv_b_fm", "A_log_r", "dt_bias_r", "D_r",
             "norm_w_r", "gate_wT", "moe_w1", "moe_w2", "moe_w3", "moe_b1_fm",
             "moe_b3_fm", "moe_b2", "sw1", "sw2", "sw3", "sb1_fm", "sb3_fm", "sb2_r",
             "Wk1", "Wq1", "Wo1", "bo_r", "WrT", "WkT", "WvT", "mu_r_fm", "mu_k_fm",
             "ident32", "U128", "TRIU32", "SLT32", "ones_row", "ones_col",
             "identbf", "SLT")
    wmap = {k: w[k] for k in wkeys}
    maps = []
    for ci in cores:
        b, par = ci // 2, ci % 2
        u_c = np.ascontiguousarray(u[b, par * T:(par + 1) * T])
        uh = (np.ascontiguousarray(u[b, T - 3:T]) if par == 1
              else np.zeros((3, D), np.float32))
        smask = np.full((128, 1), float(par), np.float32)
        maps.append(dict(u=u_c, uh=uh, smask=smask, **wmap))
    res = run_bass_kernel_spmd(nc, maps, core_ids=cores)
    out = np.empty((4, 4096, D), np.float32)
    for ci in cores:
        b, par = ci // 2, ci % 2
        out[b, par * T:(par + 1) * T] = res.results[ci]["out"]
    return out


def kernel_unfused(**inputs):
    w = prep_weights(inputs)
    u = np.ascontiguousarray(np.asarray(inputs["L0_attn_u"], dtype=np.float32))
    cores = list(range(8))
    u_c, uh_c = [], []
    for ci in cores:
        b, par = ci // 2, ci % 2
        u_c.append(np.ascontiguousarray(u[b, par * T:(par + 1) * T]))
        if par == 1:
            uh_c.append(np.ascontiguousarray(u[b, T - 3:T]))
        else:
            uh_c.append(np.zeros((3, D), np.float32))

    wA = {k: w[k] for k in ("W_inT", "conv_w_fm", "conv_b_fm", "A_log_r", "dt_bias_r",
                            "ident32", "U128", "TRIU32", "SLT32", "ones_row", "ones_col",
                            "identbf", "SLT")}
    ncA = _get("A", build_A)
    mapsA = [dict(u=u_c[ci], uh=uh_c[ci], **wA) for ci in cores]
    resA = run_bass_kernel_spmd(ncA, mapsA, core_ids=cores)
    S_in = []
    for ci in cores:
        if ci % 2 == 0:
            S_in.append(np.zeros((128, DI), np.float32))
        else:
            S_in.append(np.ascontiguousarray(resA.results[ci - 1]["S_out"]))

    wB = {k: w[k] for k in ("W_inT", "W_outT", "conv_w_fm", "conv_b_fm", "A_log_r",
                            "dt_bias_r", "D_r", "norm_w_r", "gate_wT", "moe_w1", "moe_w2",
                            "moe_w3", "moe_b1_fm", "moe_b3_fm", "moe_b2", "sw1", "sw2",
                            "sw3", "sb1_fm", "sb3_fm", "sb2_r", "Wk1",
                            "ident32", "U128", "TRIU32", "SLT32", "ones_row", "ones_col",
                            "identbf", "SLT")}
    ncB = _get("B", build_B)
    mapsB = [dict(u=u_c[ci], uh=uh_c[ci], S_in=S_in[ci], **wB) for ci in cores]
    resB = run_bass_kernel_spmd(ncB, mapsB, core_ids=cores)
    x1 = [np.ascontiguousarray(resB.results[ci]["x1"]) for ci in cores]
    K_own = [np.asarray(resB.results[ci]["K_own"]) for ci in cores]

    wC = {k: w[k] for k in ("Wq1", "Wo1", "bo_r", "WrT", "WkT", "WvT", "mu_r_fm",
                            "mu_k_fm", "ident32", "ones_row", "ones_col", "identbf")}
    ncC = _get("C", build_C)
    mapsC = []
    for ci in cores:
        b, par = ci // 2, ci % 2
        K_pair = np.ascontiguousarray(
            np.concatenate([K_own[2 * b], K_own[2 * b + 1]], axis=0))
        xprev = (x1[2 * b][T - 1:T] if par == 1 else np.zeros((1, D), np.float32))
        mapsC.append(dict(x1=x1[ci], xprev=np.ascontiguousarray(xprev), K_pair=K_pair, **wC))
    resC = run_bass_kernel_spmd(ncC, mapsC, core_ids=cores)

    out = np.empty((4, 4096, D), np.float32)
    for ci in cores:
        b, par = ci // 2, ci % 2
        out[b, par * T:(par + 1) * T] = resC.results[ci]["out"]
    return out
